# version 32
# baseline (speedup 1.0000x reference)
import numpy as np
import concourse.bass as bass
import concourse.mybir as mybir
from concourse.bass_utils import run_bass_kernel_spmd
from contextlib import ExitStack

F32 = mybir.dt.float32
BF16 = mybir.dt.bfloat16
I16 = mybir.dt.int16
I8 = mybir.dt.int8
AF = mybir.ActivationFunctionType
ALU = mybir.AluOpType
AX = mybir.AxisListType

T = 2048
D = 1024
NP = 8
NTT = T // 128
NCT = D // 128
QC = 512
NQC = T // QC
I_FF = 2816
NIT = I_FF // 128
EPS = 1e-5
M_HALF = 6291456.0
M_ONE = 12582912.0
M_TWO = 25165824.0


class Sem:
    def __init__(self, nc, es, name):
        self.h = es.enter_context(nc.semaphore(name))
        self.n = 0


class Prog:
    ENGS = ("tensor", "vector", "scalar", "gpsimd", "sync")

    def __init__(self):
        self.ops = {e: [] for e in self.ENGS}

    def wait(self, eng, sem, val=None):
        v = sem.n if val is None else val
        if v <= 0:
            return
        self.ops[eng].append(lambda e, s=sem.h, v=v: e.wait_ge(s, v))

    def do(self, eng, fn, sem=None, k=1):
        if sem is not None:
            sem.n += k
            self.ops[eng].append(lambda e, f=fn, s=sem.h, kk=k: f(e).then_inc(s, kk))
        else:
            self.ops[eng].append(fn)

    def chain(self, eng, fn, sem):
        self.wait(eng, sem)
        self.do(eng, fn, sem, 1)


def build_program(debug=False):
    nc = bass.Bass("TRN2", target_bir_lowering=False)

    x_hbm = nc.dram_tensor("x", [T, D], F32, kind="ExternalInput").ap()
    wqT = nc.dram_tensor("wqT", [D, D], F32, kind="ExternalInput").ap()
    wkT = nc.dram_tensor("wkT", [D, D], F32, kind="ExternalInput").ap()
    wvT = nc.dram_tensor("wvT", [D, D], F32, kind="ExternalInput").ap()
    woT_h = nc.dram_tensor("woT", [D, D], F32, kind="ExternalInput").ap()
    bq_h = nc.dram_tensor("bq_pp", [128, NCT], F32, kind="ExternalInput").ap()
    bk_h = nc.dram_tensor("bk_pp", [128, NCT], F32, kind="ExternalInput").ap()
    bv_h = nc.dram_tensor("bv_row", [1, D], F32, kind="ExternalInput").ap()
    bo_h = nc.dram_tensor("bo_row", [1, D], F32, kind="ExternalInput").ap()
    alpha_h = nc.dram_tensor("alpha", [1, 1], F32, kind="ExternalInput").ap()
    wg_h = nc.dram_tensor("wg", [I_FF, D], F32, kind="ExternalInput").ap()
    wu_h = nc.dram_tensor("wu", [I_FF, D], F32, kind="ExternalInput").ap()
    wd_h = nc.dram_tensor("wd", [D, I_FF], F32, kind="ExternalInput").ap()
    out_hbm = nc.dram_tensor("out", [T, D], F32, kind="ExternalOutput").ap()
    x1_dram = nc.dram_tensor("x1_scr", [T, D], F32, kind="Internal").ap()
    wdq_dram = nc.dram_tensor("wdq_scr", [D, I_FF], BF16, kind="Internal").ap()
    wgq_dram = nc.dram_tensor("wgq_scr", [I_FF, D], BF16, kind="Internal").ap()
    hqT_dram = nc.dram_tensor("hqT_scr", [NTT, 128, I_FF], BF16, kind="Internal").ap()

    dbg = {}
    if debug:
        for nm, shp, dt in (("d_qt", [D, T], BF16), ("d_kt", [D, T], BF16),
                            ("d_v", [T, D], BF16), ("d_attnT", [D, T], BF16),
                            ("d_x1", [T, D], F32), ("d_x1q", [T, D], BF16),
                            ("d_wgq", [I_FF, D], BF16), ("d_hq", [T, I_FF], BF16)):
            dbg[nm] = nc.dram_tensor(nm, shp, dt, kind="ExternalOutput").ap()

    p = Prog()

    with ExitStack() as es:
        dS = Sem(nc, es, "dS")
        dG = Sem(nc, es, "dG")
        dA = Sem(nc, es, "dA")
        dV = Sem(nc, es, "dV")
        tXT = Sem(nc, es, "tXT")
        tX1 = [Sem(nc, es, "tX1a"), Sem(nc, es, "tX1b")]
        tW = [Sem(nc, es, "tWa"), Sem(nc, es, "tWb")]
        tH = [Sem(nc, es, "tHa"), Sem(nc, es, "tHb")]
        tD = [Sem(nc, es, "tDa"), Sem(nc, es, "tDb")]
        dXR = [Sem(nc, es, "dXRa"), Sem(nc, es, "dXRb"), Sem(nc, es, "dXRc")]
        dXS = [Sem(nc, es, "dXSa"), Sem(nc, es, "dXSb"), Sem(nc, es, "dXSc")]
        dOS = [Sem(nc, es, "dOSa"), Sem(nc, es, "dOSb")]
        dWQ = [Sem(nc, es, "dWQa"), Sem(nc, es, "dWQb")]
        dHS = [Sem(nc, es, "dHSa"), Sem(nc, es, "dHSb")]
        sT = Sem(nc, es, "sT")
        sV = Sem(nc, es, "sV")
        sA = Sem(nc, es, "sA")
        sG = Sem(nc, es, "sG")

        bqs = es.enter_context(nc.sbuf_tensor("bqs", [128, NCT], F32))
        bks = es.enter_context(nc.sbuf_tensor("bks", [128, NCT], F32))
        bvb = es.enter_context(nc.sbuf_tensor("bvb", [1, D], BF16))
        bob = es.enter_context(nc.sbuf_tensor("bob", [1, D], BF16))
        ones_bf = es.enter_context(nc.sbuf_tensor("ones_bf", [1, 128], BF16))
        ones64 = es.enter_context(nc.sbuf_tensor("ones64", [128, 64], BF16))
        ones_f = es.enter_context(nc.sbuf_tensor("ones_f", [1, 128], F32))
        alpha_sb = es.enter_context(nc.sbuf_tensor("alpha_sb", [1, 1], F32))
        aneg = es.enter_context(nc.sbuf_tensor("aneg", [128, 1], F32))
        small = es.enter_context(nc.sbuf_tensor("small", [128, 16], F32))
        junk_t = es.enter_context(nc.sbuf_tensor("junk_t", [128, 16], BF16))
        idn = es.enter_context(nc.sbuf_tensor("idn", [128, 128], BF16))
        idnf = es.enter_context(nc.sbuf_tensor("idnf", [128, 128], F32))

        def quant_stages(src_ap, F, q, out_ap):
            F2 = F // 2
            nb2 = F2 // 16
            halves = []
            for h in range(2):
                c0 = h * F2
                r = q["r"][:, h * nb2:(h + 1) * nb2]
                s = q["s"][:, h * nb2:(h + 1) * nb2]
                halves.append(dict(
                    src3=src_ap[:, c0:c0 + F2].rearrange("p (b k) -> p b k", k=16),
                    r=r, s=s,
                    rb=r.unsqueeze(2).broadcast_to([128, nb2, 16]),
                    sb=s.unsqueeze(2).broadcast_to([128, nb2, 16]),
                    xn=q["xn"][:, c0:c0 + F2], c1=q["c1"][:, c0:c0 + F2],
                    c2=q["c2"][:, c0:c0 + F2], cc=q["cc"][:, c0:c0 + F2],
                    m=q["m"][:, c0:c0 + F2],
                    xn3=q["xn"][:, c0:c0 + F2].rearrange("p (b k) -> p b k", k=16),
                    c13=q["c1"][:, c0:c0 + F2].rearrange("p (b k) -> p b k", k=16),
                    out3=out_ap[:, c0:c0 + F2].rearrange("p (b k) -> p b k", k=16)))
            C = lambda fn: p.chain("vector", fn, sV)
            A = lambda fn: p.chain("scalar", fn, sA)
            a_sq1, a_sq2 = {}, {}

            def front(h):
                d = halves[h]
                C(lambda e, d=d: e.tensor_reduce(
                    out=d["r"], in_=d["src3"], axis=AX.X, op=ALU.max,
                    apply_absolute_value=True))
                C(lambda e, d=d: e.tensor_scalar(
                    out=d["s"], in0=d["r"], scalar1=1.0 / 6.0, scalar2=1e-12,
                    op0=ALU.mult, op1=ALU.max))
                C(lambda e, d=d: e.reciprocal(out=d["r"], in_=d["s"]))
                C(lambda e, d=d: e.tensor_tensor(
                    out=d["xn3"], in0=d["src3"], in1=d["rb"], op=ALU.mult))
                C(lambda e, d=d: e.tensor_scalar(
                    out=d["c1"], in0=d["xn"], scalar1=M_HALF, scalar2=M_HALF,
                    op0=ALU.add, op1=ALU.subtract))
                p.wait("scalar", sV)
                A(lambda e, d=d: e.activation(out=d["cc"], in_=d["c1"], func=AF.Square))
                a_sq1[h] = sA.n
                C(lambda e, d=d: e.tensor_scalar(
                    out=d["c2"], in0=d["xn"], scalar1=M_ONE, scalar2=M_ONE,
                    op0=ALU.add, op1=ALU.subtract))

            def mid(h):
                d = halves[h]
                p.wait("vector", sA, a_sq1[h])
                C(lambda e, d=d: e.tensor_scalar(
                    out=d["m"], in0=d["cc"], scalar1=6.25, scalar2=None, op0=ALU.is_ge))
                p.wait("scalar", sV)
                A(lambda e, d=d: e.activation(out=d["cc"], in_=d["c2"], func=AF.Square))
                a_sq2[h] = sA.n
                C(lambda e, d=d: e.copy_predicated(out=d["c1"], mask=d["m"], data=d["c2"]))

            def back(h):
                d = halves[h]
                p.wait("vector", sA, a_sq2[h])
                C(lambda e, d=d: e.tensor_scalar(
                    out=d["m"], in0=d["cc"], scalar1=25.0, scalar2=None, op0=ALU.is_ge))
                C(lambda e, d=d: e.tensor_scalar(
                    out=d["c2"], in0=d["xn"], scalar1=M_TWO, scalar2=M_TWO,
                    op0=ALU.add, op1=ALU.subtract))
                C(lambda e, d=d: e.copy_predicated(out=d["c1"], mask=d["m"], data=d["c2"]))
                C(lambda e, d=d: e.tensor_tensor(
                    out=d["out3"], in0=d["c13"], in1=d["sb"], op=ALU.mult))

            return [lambda: front(0), lambda: front(1), lambda: mid(0),
                    lambda: mid(1), lambda: back(0), lambda: back(1)]

        def emit_quant(src_ap, F, q, out_ap):
            for st in quant_stages(src_ap, F, q, out_ap):
                st()

        QUEUES = ("scalar", "sync")
        QSEM = {"scalar": dA, "gpsimd": dG, "sync": dS, "vector": dV}

        def spread_transposes(pairs, sem):
            for (o, i) in pairs:
                p.do("sync", lambda e, o=o, i=i: e.dma_start_transpose(out=o, in_=i),
                     sem, 16)
            return sem.n

        pe_tp_state = {}

        def emit_pe_tp(groups, ps_tp):
            slot_free = pe_tp_state.setdefault(id(ps_tp), [0, 0])
            for gi, (dst, srcs) in enumerate(groups):
                sl = gi % 2
                p.wait("tensor", sV, slot_free[sl])
                for si, s_ in enumerate(srcs):
                    p.do("tensor", lambda e, sl=sl, si=si, s_=s_: e.transpose(
                        out=ps_tp.ap()[:, sl * 1024 + si * 128:sl * 1024 + (si + 1) * 128],
                        in_=s_, identity=idn[:, :]), sT, 1)
                p.wait("vector", sT, sT.n)
                p.do("vector", lambda e, sl=sl, dst=dst, n=len(srcs): e.tensor_copy(
                    out=dst, in_=ps_tp.ap()[:, sl * 1024:sl * 1024 + n * 128]), sV, 1)
                slot_free[sl] = sV.n
            return sT.n, sV.n

        def wait_counts(eng, counts):
            for qn, v in counts.items():
                p.wait(eng, QSEM[qn], v)

        p.do("sync", lambda e: e.dma_start(out=bqs[:, :], in_=bq_h[:, :]), dS, 16)
        p.do("sync", lambda e: e.dma_start(out=bks[:, :], in_=bk_h[:, :]), dS, 16)
        p.do("sync", lambda e: e.dma_start(out=alpha_sb[:, :], in_=alpha_h[:, :]), dS, 16)
        dS_small = dS.n
        p.do("gpsimd", lambda e: e.dma_start(out=bvb[:, :], in_=bv_h[:, :]), dG, 16)
        p.do("gpsimd", lambda e: e.dma_start(out=bob[:, :], in_=bo_h[:, :]), dG, 16)
        p.do("vector", lambda e: e.memset(ones_bf[:, :], 1.0))
        p.do("vector", lambda e: e.memset(ones64[:, :], 1.0))
        p.do("vector", lambda e: e.memset(junk_t[:, :], 0.0))
        p.do("vector", lambda e: e.memset(ones_f[:, :], 1.0), sV, 1)
        sV_ones = sV.n
        p.do("gpsimd", lambda e: e.iota(
            out=idnf[:, :], pattern=[[-1, 128]], base=0, channel_multiplier=1,
            allow_small_or_imprecise_dtypes=True), sG, 1)
        p.wait("vector", sG, sG.n)
        p.do("vector", lambda e: e.tensor_scalar(
            out=idn[:, :], in0=idnf[:, :], scalar1=0.0, scalar2=None,
            op0=ALU.is_equal), sV, 1)
        sV_idn = sV.n
        p.wait("vector", dS, dS_small)
        p.do("vector", lambda e: e.tensor_scalar(
            out=bqs[:, :], in0=bqs[:, :], scalar1=0.125, scalar2=None, op0=ALU.mult), sV, 1)

        with nc.sbuf_tensor("QT", [128, NCT * T], BF16) as QT, \
             nc.sbuf_tensor("KT", [128, NCT * T], BF16) as KT, \
             nc.sbuf_tensor("VV", [128, NTT * D], BF16) as VV, \
             nc.sbuf_tensor("Wd", [128, 3968], F32) as Wd:

            QTa = QT.ap().rearrange("p (c t) -> p c t", t=T)
            KTa = KT.ap().rearrange("p (c t) -> p c t", t=T)
            VVa = VV.ap().rearrange("p (k d) -> p k d", d=D)

            with nc.sbuf_tensor("XT", [128, NCT * T], BF16) as XT:
                XTa = XT.ap().rearrange("p (c t) -> p c t", t=T)
                with nc.sbuf_tensor("wbuf", [128, NCT * D], BF16) as wbuf, \
                     nc.sbuf_tensor("xb", [128, NTT * D], BF16) as xb:
                    ps_al_cm = nc.psum_tensor("ps_al", [128, 1], F32)
                    ps_al = ps_al_cm.__enter__()
                    xba = xb.ap().rearrange("p (t d) -> p t d", d=D)
                    x_r = x_hbm.rearrange("(t p) d -> p t d", p=128)
                    p.do("gpsimd", lambda e: e.dma_start(out=xba, in_=x_r), dG, 16)
                    dG_xb = dG.n
                    p.do("gpsimd", lambda e: e.iota(
                        out=Wd[:, :], pattern=[[-1, 3968]], base=1920,
                        channel_multiplier=1, allow_small_or_imprecise_dtypes=True), sG, 1)
                    p.wait("tensor", dS, dS_small)
                    p.wait("tensor", sV, sV_ones)
                    p.do("tensor", lambda e: e.matmul(
                        out=ps_al[:, :], lhsT=ones_f[:, :], rhs=alpha_sb[:, :],
                        start=True, stop=True), sT, 1)
                    p.wait("vector", sT, sT.n)
                    p.do("vector", lambda e: e.tensor_scalar(
                        out=aneg[:, :], in0=ps_al[:, :], scalar1=-1.0, scalar2=None,
                        op0=ALU.mult), sV, 1)
                    p.wait("vector", sG, sG.n)
                    p.do("vector", lambda e: e.scalar_tensor_tensor(
                        out=Wd[:, :], in0=Wd[:, :], scalar=-1.0, in1=Wd[:, :],
                        op0=ALU.mult, op1=ALU.max), sV, 1)
                    p.wait("scalar", sV, sV.n)
                    p.do("scalar", lambda e: e.activation(
                        out=Wd[:, :], in_=Wd[:, :], func=AF.Exp,
                        scale=aneg.ap()[:, 0:1]), sA, 1)
                    sA_wd = sA.n
                    sV_prep = sV.n
                    with nc.psum_tensor("ps_tp0", [128, 2048], BF16) as ps_tp0:
                        p.wait("tensor", dG, dG_xb)
                        p.wait("tensor", sV, sV_idn)
                        xt_groups = [(XTa[:, j, tt0 * 128:(tt0 + 4) * 128],
                                      [xba[:, tt, j * 128:(j + 1) * 128]
                                       for tt in range(tt0, tt0 + 4)])
                                     for j in range(NCT) for tt0 in (0, 4, 8, 12)]
                        _, xt_done = emit_pe_tp(xt_groups, ps_tp0)

                    ps_al_cm.__exit__(None, None, None)
                    ph1 = nc.psum_tensor("ps_q", [128, 2 * T], F32)
                    ps_q = ph1.__enter__()
                    wba = wbuf.ap().rearrange("p (c d) -> p c d", d=D)
                    psq = ps_q.ap().rearrange("p (g t) -> p g t", t=T)
                    evac_done = {0: 0, 1: 0}

                    def load_w(src, after_sT):
                        p.wait("gpsimd", sT, after_sT)
                        p.wait("gpsimd", tXT)
                        p.wait("gpsimd", dG, dG.n)
                        wr = src.rearrange("(c p) d -> p c d", p=128)
                        p.do("gpsimd", lambda e, wr=wr: e.dma_start(out=wba, in_=wr), dG, 16)
                        return dG.n

                    dG_w = load_w(wqT, 0)
                    p.wait("tensor", sV, max(sV_prep, xt_done))
                    for which, (bias_sb, scl, outTa) in enumerate(
                            ((bqs, 0.125, QTa), (bks, 1.0, KTa))):
                        p.wait("tensor", dG, dG_w)
                        for j in range(NCT):
                            g = j % 2
                            if evac_done[g]:
                                p.wait("tensor", sA, evac_done[g])
                            for i in range(NCT):
                                for q4 in range(4):
                                    p.do("tensor", lambda e, i=i, j=j, q4=q4, g=g: e.matmul(
                                        out=psq[:, g, q4 * 512:(q4 + 1) * 512],
                                        lhsT=wba[:, i, j * 128:(j + 1) * 128],
                                        rhs=XTa[:, i, q4 * 512:(q4 + 1) * 512],
                                        start=(i == 0), stop=(i == NCT - 1)),
                                        sT if (i == NCT - 1 and q4 == 3) else None, 1)
                            p.wait("scalar", sT, sT.n)
                            p.do("scalar", lambda e, j=j, g=g, b=bias_sb, s=scl, o=outTa:
                                 e.activation(out=o[:, j, :], in_=psq[:, g, :],
                                              func=AF.Identity, bias=b.ap()[:, j:j + 1],
                                              scale=s), sA, 1)
                            evac_done[g] = sA.n
                        dG_w = load_w(wkT if which == 0 else wvT, sT.n)
                    p.wait("tensor", dG, dG_w)
                    for tt in range(NTT):
                        g = tt % 2
                        if evac_done[g]:
                            p.wait("tensor", sA, evac_done[g])
                        for i in range(NCT):
                            for c in range(2):
                                p.do("tensor", lambda e, tt=tt, i=i, c=c, g=g: e.matmul(
                                    out=psq[:, g, c * 512:(c + 1) * 512],
                                    lhsT=XTa[:, i, tt * 128:(tt + 1) * 128],
                                    rhs=wba[:, i, c * 512:(c + 1) * 512],
                                    start=(i == 0), stop=False))
                        for c in range(2):
                            p.do("tensor", lambda e, c=c, g=g: e.matmul(
                                out=psq[:, g, c * 512:(c + 1) * 512],
                                lhsT=ones_bf[:, :], rhs=bvb.ap()[:, c * 512:(c + 1) * 512],
                                start=False, stop=True), sT if c == 1 else None, 1)
                        p.wait("scalar", sT, sT.n)
                        p.do("scalar", lambda e, tt=tt, g=g: e.activation(
                            out=VVa[:, tt, 0:D], in_=psq[:, g, 0:D], func=AF.Identity), sA, 1)
                        evac_done[g] = sA.n
                    ph1.__exit__(None, None, None)
            sA_proj = sA.n

            if debug:
                p.wait("gpsimd", sA, sA_proj)
                p.do("gpsimd", lambda e: e.dma_start(
                    out=dbg["d_qt"].rearrange("(c p) t -> p c t", p=128), in_=QTa), dG, 16)
                p.do("gpsimd", lambda e: e.dma_start(
                    out=dbg["d_kt"].rearrange("(c p) t -> p c t", p=128), in_=KTa), dG, 16)
                p.do("gpsimd", lambda e: e.dma_start(
                    out=dbg["d_v"].rearrange("(t p) d -> p t d", p=128), in_=VVa), dG, 16)

            with nc.sbuf_tensor("SDr1", [128, 8 * 512], F32) as SDr1, \
                 nc.sbuf_tensor("SDr2", [128, 8 * 512], F32) as SDr2, \
                 nc.sbuf_tensor("PT1", [128, NTT * QC], BF16) as PT1, \
                 nc.sbuf_tensor("PT2", [128, NTT * QC], BF16) as PT2, \
                 nc.sbuf_tensor("recD", [128, QC], F32) as recD, \
                 nc.psum_tensor("ps_s1", [128, 3 * 512], F32) as ps_s1, \
                 nc.psum_tensor("ps_s2", [128, 3 * 512], F32) as ps_s2, \
                 nc.psum_tensor("ps_av", [128, 512], F32) as ps_av, \
                 nc.psum_tensor("ps_d", [128, 512], F32) as ps_d:

                p.wait("tensor", sA, sA_proj)
                aq_es = ExitStack()
                aq_sb = lambda n, sh, dt: aq_es.enter_context(nc.sbuf_tensor(n, sh, dt))
                awst = aq_sb("awst", [128, 1408], F32)
                awq = aq_sb("awq", [128, 1408], BF16)
                aqxn = aq_sb("aqxn", [128, 1408], F32)
                aqc1 = aq_sb("aqc1", [128, 1408], BF16)
                aqc2 = aq_sb("aqc2", [128, 1408], BF16)
                aqcc = aq_sb("aqcc", [128, 1408], BF16)
                aqm = aq_sb("aqm", [128, 1408], I8)
                aqr = aq_sb("aqr", [128, 88], F32)
                aqs = aq_sb("aqs", [128, 88], F32)

                aqb = {"r": aqr.ap(), "s": aqs.ap(), "xn": aqxn.ap(),
                       "c1": aqc1.ap(), "c2": aqc2.ap(), "cc": aqcc.ap(),
                       "m": aqm.ap()}
                wd_r2 = wd_h.rearrange("(r p) i -> p r i", p=128)
                wdq_r2 = wdq_dram.rearrange("(r p) i -> p r i", p=128)
                wg_r2 = wg_h.rearrange("(r p) c -> p r c", p=128)
                wgq_r2 = wgq_dram.rearrange("(r p) c -> p r c", p=128)
                aw_chunks = []
                for rt in range(8):
                    for hf in range(2):
                        aw_chunks.append((wd_r2[:, rt, hf * 1408:(hf + 1) * 1408],
                                          wdq_r2[:, rt, hf * 1408:(hf + 1) * 1408],
                                          1408))
                for row in range(8):
                    aw_chunks.append((wg_r2[:, row, :], wgq_r2[:, row, :], 1024))
                aw_load = {}
                aw_read = {}
                aw_store = {}
                p.wait("sync", dG, dG.n)
                p.wait("sync", sT, sT.n)
                p.wait("sync", sV, sV.n)
                aw_sched = {}

                def aw_actions(k):
                    src, dst, Fk = aw_chunks[k]
                    stg = quant_stages(awst.ap()[:, :Fk], Fk, aqb, awq.ap()[:, :Fk])

                    def a_load():
                        if k >= 1:
                            p.wait("sync", sV, aw_read[k - 1])
                        p.wait("sync", dS, dS.n)
                        p.do("sync", lambda e, src=src, Fk=Fk: e.dma_start(
                            out=awst.ap()[:, :Fk], in_=src), dS, 16)
                        aw_load[k] = dS.n

                    def a_f0():
                        p.wait("vector", dS, aw_load[k])
                        stg[0]()

                    def a_f1():
                        stg[1]()
                        aw_read[k] = sV.n

                    def a_b0():
                        if k >= 1:
                            p.wait("vector", dWQ[0], aw_store[k - 1])
                        stg[4]()

                    def a_store():
                        p.wait("gpsimd", sV, sV.n)
                        p.do("gpsimd", lambda e, dst=dst, Fk=Fk: e.dma_start(
                            out=dst, in_=awq.ap()[:, :Fk]), dWQ[0], 16)
                        aw_store[k] = dWQ[0].n

                    return [a_load, a_f0, a_f1, stg[2], stg[3], a_b0, stg[5],
                            a_store]

                for k in range(len(aw_chunks)):
                    acts = aw_actions(k)
                    base = k * 21
                    for oi, off in enumerate((0, 2, 4, 6, 8, 10, 12, 13)):
                        aw_sched[base + off] = acts[oi]
                wd_stride = 3968
                units = [(hp, qc) for hp in range(NP) for qc in range(NQC)]
                sd_kt = {0: 0, 1: 0, 2: 0}
                exp_ring = {0: 0, 1: 0}
                exps_grp = {}
                evac_done = {}
                recd_done = {}
                first_sd = True

                def emit_av_kt(up, kt):
                    hp_, _ = units[up]
                    for half, (PT, rq0) in enumerate(((PT1, 0), (PT2, 64))):
                        h = 2 * hp_ + half
                        p.do("tensor", lambda e, kt=kt, PT=PT, rq0=rq0, h=h: e.matmul(
                            out=ps_av.ap()[rq0:rq0 + 64, :],
                            lhsT=VVa[:, kt, h * 64:(h + 1) * 64],
                            rhs=PT.ap()[:, kt * QC:(kt + 1) * QC],
                            start=(kt == 0), stop=(kt == NTT - 1),
                            skip_group_check=True,
                            tile_position=(0, rq0)))
                        p.do("tensor", lambda e, kt=kt, PT=PT, rq0=rq0: e.matmul(
                            out=ps_d.ap()[rq0:rq0 + 64, :],
                            lhsT=ones64[:, :],
                            rhs=PT.ap()[:, kt * QC:(kt + 1) * QC],
                            start=(kt == 0), stop=(kt == NTT - 1),
                            skip_group_check=True,
                            tile_position=(0, rq0)),
                            sT if half == 1 else None, 1)

                for u in range(len(units) + 1):
                    if u < len(units):
                        hp, qc = units[u]
                        for kt in range(NTT):
                            b2 = kt % 3
                            r8 = (kt // 4) % 2
                            p.wait("tensor", sV, sd_kt[b2])
                            for half, (ps, rq0) in enumerate(((ps_s1, 0), (ps_s2, 64))):
                                p.do("tensor",
                                     lambda e, hp=hp, qc=qc, kt=kt, b2=b2, ps=ps, rq0=rq0:
                                     e.matmul(
                                         out=ps.ap()[:, b2 * 512:b2 * 512 + QC],
                                         lhsT=KTa[rq0:rq0 + 64, hp, kt * 128:(kt + 1) * 128],
                                         rhs=QTa[rq0:rq0 + 64, hp, qc * QC:(qc + 1) * QC],
                                         start=True, stop=True,
                                         tile_position=(rq0, 0)),
                                     sT if half == 1 else None, 1)
                            if u >= 1:
                                up = u - 1
                                p.wait("tensor", sA, exps_grp[(up, kt // 4)])
                                if kt == 0:
                                    p.wait("tensor", sV, evac_done.get(up - 1, 0))
                                    p.wait("tensor", sA, recd_done.get(up - 1, 0))
                                emit_av_kt(up, kt)
                            p.wait("vector", sT, sT.n)
                            if first_sd:
                                p.wait("vector", sA, sA_wd)
                                first_sd = False
                            if exp_ring[r8]:
                                p.wait("vector", sA, exp_ring[r8])
                            s8 = kt % 8
                            for hi, (ps, SDr) in enumerate(((ps_s1, SDr1), (ps_s2, SDr2))):
                                def dve_sd(e, ps=ps, SDr=SDr, b2=b2, s8=s8, kt=kt, qc=qc):
                                    base = 1920 - 128 * kt + QC * qc
                                    wd_ap = bass.AP(Wd.ap().tensor, base,
                                                    [[wd_stride, 128], [1, QC]])
                                    return e.tensor_tensor(
                                        out=SDr.ap()[:, s8 * 512:(s8 + 1) * 512],
                                        in0=ps.ap()[:, b2 * 512:b2 * 512 + QC],
                                        in1=wd_ap, op=ALU.mult)
                                p.do("vector", dve_sd, sV, 1)
                            sd_kt[b2] = sV.n
                            if kt % 4 == 3:
                                g = kt // 4
                                p.wait("scalar", sV, sV.n)
                                p.wait("scalar", sT, sT.n)
                                for hi, (SDr, PT) in enumerate(((SDr1, PT1), (SDr2, PT2))):
                                    p.do("scalar", lambda e, SDr=SDr, PT=PT, g=g, r8=r8:
                                         e.activation(
                                             out=PT.ap()[:, 4 * g * QC:(4 * g + 4) * QC],
                                             in_=SDr.ap()[:, r8 * 2048:(r8 + 1) * 2048],
                                             func=AF.Exp), sA, 1)
                                exp_ring[r8] = sA.n
                                exps_grp[(u, g)] = sA.n
                            act = aw_sched.get(u * NTT + kt)
                            if act is not None:
                                act()
                    if u >= 1:
                        up = u - 1
                        hp_, qc_ = units[up]
                        if u == len(units):
                            p.wait("tensor", sA, exps_grp[(up, 3)])
                            p.wait("tensor", sV, evac_done.get(up - 1, 0))
                            p.wait("tensor", sA, recd_done.get(up - 1, 0))
                            for kt in range(NTT):
                                emit_av_kt(up, kt)
                        p.wait("scalar", sT, sT.n)
                        p.chain("scalar", lambda e: e.activation(
                            out=recD[:, :], in_=ps_d.ap()[:, :], func=AF.Ln), sA)
                        p.chain("scalar", lambda e: e.activation(
                            out=recD[:, :], in_=recD[:, :], func=AF.Exp, scale=-1.0), sA)
                        recd_done[up] = sA.n
                        p.wait("vector", sA, sA.n)
                        p.wait("vector", sT, sT.n)
                        p.do("vector", lambda e, hp_=hp_, qc_=qc_: e.tensor_tensor(
                            out=QTa[:, hp_, qc_ * QC:(qc_ + 1) * QC],
                            in0=ps_av.ap()[:, :], in1=recD[:, :], op=ALU.mult), sV, 1)
                        evac_done[up] = sV.n
                aq_es.close()
            sV_attn = sV.n
            if debug:
                p.wait("gpsimd", sV, sV_attn)
                p.do("gpsimd", lambda e: e.dma_start(
                    out=dbg["d_attnT"].rearrange("(c p) t -> p c t", p=128), in_=QTa), dG, 16)

            with nc.sbuf_tensor("woT_sb", [128, NCT * D], BF16) as woT_sb, \
                 nc.sbuf_tensor("xres", [128, 3 * D], F32) as xres, \
                 nc.sbuf_tensor("ra", [128, 3 * D], F32) as ra, \
                 nc.sbuf_tensor("sqsc", [128, D], BF16) as sqsc, \
                 nc.psum_tensor("ps_y", [128, 2 * D], F32) as ps_y:

                woTa = woT_sb.ap().rearrange("p (c d) -> p c d", d=D)
                woT_r = woT_h.rearrange("(c p) d -> p c d", p=128)
                p.wait("gpsimd", sV, sV_attn)
                p.do("gpsimd", lambda e: e.dma_start(out=woTa, in_=woT_r), dG, 16)
                dG_wo = dG.n
                xre = xres.ap().rearrange("p (g d) -> p g d", d=D)
                raa = ra.ap().rearrange("p (g d) -> p g d", d=D)
                psy = ps_y.ap().rearrange("p (g d) -> p g d", d=D)
                x_rt = x_hbm.rearrange("(t p) d -> p t d", p=128)
                x1_rt = x1_dram.rearrange("(t p) d -> p t d", p=128)

                p.wait("tensor", dG, dG_wo)
                p.wait("tensor", sV, sV_attn)
                p.wait("sync", sV, sV_attn)
                add_vals = {}
                store_vals = {}
                sq_vals = {}

                def p3_head(tt):
                    s3 = tt % 3
                    g = tt % 2
                    b = 8 * g
                    if tt >= 3:
                        p.wait("sync", sV, add_vals[tt - 3])
                    p.do("sync", lambda e, tt=tt, s3=s3: e.dma_start(
                        out=xre[:, s3, :], in_=x_rt[:, tt, :]), dXR[s3], 16)
                    dS_here = dXR[s3].n
                    if tt >= 2:
                        p.wait("tensor", sV, add_vals[tt - 2])
                    for i in range(NCT):
                        for c in range(2):
                            p.do("tensor", lambda e, tt=tt, i=i, c=c, g=g: e.matmul(
                                out=psy[:, g, c * 512:(c + 1) * 512],
                                lhsT=QTa[:, i, tt * 128:(tt + 1) * 128],
                                rhs=woTa[:, i, c * 512:(c + 1) * 512],
                                start=(i == 0), stop=False))
                    for c in range(2):
                        p.do("tensor", lambda e, c=c, g=g: e.matmul(
                            out=psy[:, g, c * 512:(c + 1) * 512],
                            lhsT=ones_bf[:, :], rhs=bob.ap()[:, c * 512:(c + 1) * 512],
                            start=False, stop=True), sT if c == 1 else None, 1)
                    p.wait("vector", sT, sT.n)
                    p.wait("vector", dXR[s3], dS_here)
                    if tt >= 3:
                        p.wait("vector", dXS[s3], store_vals[tt - 3])
                    p.do("vector", lambda e, g=g, s3=s3: e.tensor_tensor(
                        out=raa[:, s3, :], in0=psy[:, g, :], in1=xre[:, s3, :], op=ALU.add), sV, 1)
                    add_vals[tt] = sV.n
                    p.chain("vector", lambda e, s3=s3, b=b: e.tensor_reduce(
                        out=small[:, b:b + 1], in_=raa[:, s3, :], axis=AX.X, op=ALU.add), sV)
                    p.wait("scalar", sV, sV.n)
                    p.chain("scalar", lambda e, s3=s3, b=b: e.activation(
                        out=sqsc[:, :], in_=raa[:, s3, :], func=AF.Square,
                        accum_out=small[:, b + 1:b + 2]), sA)
                    sq_vals[tt] = sA.n

                def p3_tail(tt):
                    s3 = tt % 3
                    g = tt % 2
                    b = 8 * g
                    p.wait("vector", sA, sq_vals[tt])
                    p.chain("vector", lambda e, b=b: e.tensor_scalar(
                        out=small[:, b + 2:b + 3], in0=small[:, b:b + 1], scalar1=1.0 / D,
                        scalar2=None, op0=ALU.mult), sV)
                    p.chain("vector", lambda e, b=b: e.tensor_scalar(
                        out=small[:, b + 3:b + 4], in0=small[:, b + 2:b + 3], scalar1=-1.0,
                        scalar2=None, op0=ALU.mult), sV)
                    p.chain("vector", lambda e, b=b: e.tensor_scalar(
                        out=small[:, b + 5:b + 6], in0=small[:, b + 1:b + 2], scalar1=1.0 / D,
                        scalar2=EPS, op0=ALU.mult, op1=ALU.add), sV)
                    p.chain("vector", lambda e, b=b: e.scalar_tensor_tensor(
                        out=small[:, b + 4:b + 5], in0=small[:, b + 2:b + 3],
                        scalar=small.ap()[:, b + 2:b + 3],
                        in1=small[:, b + 5:b + 6], op0=ALU.mult, op1=ALU.subtract), sV)
                    p.wait("scalar", sV, sV.n)
                    p.chain("scalar", lambda e, b=b: e.activation(
                        out=small[:, b + 5:b + 6], in_=small[:, b + 4:b + 5], func=AF.Ln,
                        scale=-1.0), sA)
                    p.chain("scalar", lambda e, b=b: e.activation(
                        out=small[:, b + 6:b + 7], in_=small[:, b + 5:b + 6], func=AF.Exp,
                        scale=-0.5), sA)
                    p.chain("scalar", lambda e, b=b: e.activation(
                        out=small[:, b + 7:b + 8], in_=small[:, b + 6:b + 7], func=AF.Identity,
                        scale=small.ap()[:, b + 3:b + 4]), sA)
                    p.chain("scalar", lambda e, s3=s3, b=b: e.activation(
                        out=raa[:, s3, :], in_=raa[:, s3, :], func=AF.Identity,
                        scale=small.ap()[:, b + 6:b + 7], bias=small.ap()[:, b + 7:b + 8]), sA)
                    p.wait("scalar", sA, sA.n)
                    p.do("scalar", lambda e, tt=tt, s3=s3: e.dma_start(
                        out=x1_rt[:, tt, :], in_=raa[:, s3, :]), dXS[s3], 16)
                    store_vals[tt] = dXS[s3].n

                p3_head(0)
                for tt in range(1, NTT):
                    p3_head(tt)
                    p3_tail(tt - 1)
                p3_tail(NTT - 1)
                if debug:
                    for s_ in dXS:
                        p.wait("gpsimd", s_)
                    p.do("gpsimd", lambda e: e.dma_start(
                        out=dbg["d_x1"].rearrange("(t p) d -> p t d", p=128),
                        in_=x1_dram.rearrange("(t p) d -> p t d", p=128)), dG, 16)

        x1qT = es.enter_context(nc.sbuf_tensor("x1qT", [128, NCT * T], BF16))
        qxn = es.enter_context(nc.sbuf_tensor("qxn", [128, I_FF], F32))
        qc1 = es.enter_context(nc.sbuf_tensor("qc1", [128, I_FF], BF16))
        qc2 = es.enter_context(nc.sbuf_tensor("qc2", [128, I_FF], BF16))
        qcc = es.enter_context(nc.sbuf_tensor("qcc", [128, I_FF], BF16))
        qm = es.enter_context(nc.sbuf_tensor("qm", [128, I_FF], I8))
        qr = es.enter_context(nc.sbuf_tensor("qr", [128, I_FF // 16], F32))
        qs = es.enter_context(nc.sbuf_tensor("qs", [128, I_FF // 16], F32))
        if True:
            qb = {"r": qr.ap(), "s": qs.ap(), "xn": qxn.ap(), "c1": qc1.ap(),
                  "c2": qc2.ap(), "cc": qcc.ap(), "m": qm.ap()}
            x1qTa = x1qT.ap().rearrange("p (c t) -> p c t", t=T)
            x1_rt = x1_dram.rearrange("(t p) d -> p t d", p=128)

            with nc.sbuf_tensor("x1sl", [128, 2 * 2 * D], F32) as x1sl, \
                 nc.sbuf_tensor("x1q", [128, 2 * 2 * D], BF16) as x1q:
                x1sla = x1sl.ap().rearrange("p (q t d) -> p q t d", q=2, d=D)
                x1qa = x1q.ap().rearrange("p (q t d) -> p q t d", q=2, d=D)
                for s_ in dXS:
                    p.wait("sync", s_)
                    p.wait("scalar", s_)
                ps_tp4 = nc.psum_tensor("ps_tp4", [128, 2048], BF16)
                ps_tp4t = ps_tp4.__enter__()
                x1q_reads = [0, 0]
                x1qT_done = 0
                for sl in range(NTT // 2):
                    par = sl % 2
                    p.wait("sync", sT, x1q_reads[par])
                    p.wait("sync", dS, dS.n)
                    p.do("sync", lambda e, sl=sl, par=par: e.dma_start(
                        out=x1sla[:, par, :, :],
                        in_=x1_rt[:, 2 * sl:2 * sl + 2, :]), dS, 16)
                    p.wait("vector", dS, dS.n)
                    p.wait("vector", sT, x1q_reads[par])
                    emit_quant(x1sl.ap()[:, par * 2 * D:(par + 1) * 2 * D], 2 * D, qb,
                               x1q.ap()[:, par * 2 * D:(par + 1) * 2 * D])
                    p.wait("tensor", sV, sV.n)
                    x1q_reads[par], x1qT_done = emit_pe_tp(
                        [(x1qTa[:, j, 2 * sl * 128:(2 * sl + 2) * 128],
                          [x1qa[:, par, t2, j * 128:(j + 1) * 128] for t2 in range(2)])
                         for j in range(NCT)], ps_tp4t)
                    if debug:
                        p.wait("gpsimd", sV, sV.n)
                        p.do("gpsimd", lambda e, sl=sl, par=par: e.dma_start(
                            out=dbg["d_x1q"].rearrange("(t p) d -> p t d", p=128)[:, 2 * sl:2 * sl + 2, :],
                            in_=x1qa[:, par, :, :]), dG, 16)

            for eng in ("sync", "vector", "scalar", "gpsimd"):
                p.wait(eng, sV, x1qT_done)
            ps_tp4.__exit__(None, None, None)
            with nc.sbuf_tensor("wgT", [128, NCT * I_FF], BF16) as wgT, \
                 nc.sbuf_tensor("wuT", [128, NCT * I_FF], BF16) as wuT:
                wgTa = wgT.ap().rearrange("p (c i) -> p c i", i=I_FF)
                wuTa = wuT.ap().rearrange("p (c i) -> p c i", i=I_FF)

                with nc.sbuf_tensor("wst", [128, 2 * I_FF], F32) as wst, \
                     nc.sbuf_tensor("wqo", [128, 2 * I_FF], BF16) as wqo:
                    p.wait("sync", dWQ[0])
                    q_end = {0: 0, 1: 0}
                    cidx = 0
                    ps_tp5 = nc.psum_tensor("ps_tp5", [128, 2048], BF16)
                    ps_tp5t = ps_tp5.__enter__()
                    wqo_reads = [0, 0]
                    wqo_dmat = [0, 0]
                    pend_tp = None
                    wT_done = 0
                    wgq_r = wgq_dram.rearrange("(r p) c -> p r c", p=128)
                    for w_src, wTa, preq, dbgk in ((wg_h, wgTa, 4, "d_wgq"),
                                                   (wu_h, wuTa, 0, None)):
                        w_r = w_src.rearrange("(r p) c -> p r c", p=128)
                        for ch in range(11):
                            par = cidx % 2
                            cidx += 1
                            wqop = wqo.ap()[:, par * I_FF:par * I_FF + 2 * D]
                            wq3 = wqop.rearrange("p (r c) -> p r c", c=D)
                            if ch < preq:
                                p.wait("gpsimd", sT, wqo_reads[par])
                                p.wait("gpsimd", tW[par], wqo_dmat[par])
                                p.wait("gpsimd", dG, dG.n)
                                p.do("gpsimd", lambda e, ch=ch, wq3=wq3: e.dma_start(
                                    out=wq3, in_=wgq_r[:, 2 * ch:2 * ch + 2, :]), dG, 16)
                                p.wait("tensor", dG, dG.n)
                            else:
                                p.wait("gpsimd", sV, q_end[par])
                                p.wait("gpsimd", dG, dG.n)
                                p.do("gpsimd", lambda e, ch=ch, w_r=w_r, par=par: e.dma_start(
                                    out=wst.ap()[:, par * I_FF:par * I_FF + 2 * D].rearrange(
                                        "p (r c) -> p r c", c=D),
                                    in_=w_r[:, 2 * ch:2 * ch + 2, :]), dG, 16)
                                p.wait("vector", dG, dG.n)
                                p.wait("vector", sT, wqo_reads[par])
                                p.wait("vector", tW[par], wqo_dmat[par])
                                emit_quant(wst.ap()[:, par * I_FF:par * I_FF + 2 * D],
                                           2 * D, qb, wqop)
                                q_end[par] = sV.n
                                p.wait("tensor", sV, sV.n)
                            if pend_tp is not None:
                                pend_tp()
                            qv = q_end[par]
                            dv = dG.n

                            def mk_pend(par=par, wTa=wTa, wq3=wq3, ch=ch,
                                        qv=qv, dv=dv, direct=(ch < preq)):
                                def go():
                                    if direct:
                                        p.wait("sync", dG, dv)
                                    else:
                                        p.wait("sync", sV, qv)
                                    wqo_dmat[par] = spread_transposes(
                                        [(wTa[:, j, (2 * ch + r2) * 128:
                                               (2 * ch + r2 + 1) * 128],
                                          wq3[:, r2, j * 128:(j + 1) * 128])
                                         for j in range(4) for r2 in range(2)],
                                        tW[par])
                                return go
                            pend_tp = mk_pend()
                            wqo_reads[par], wT_done = emit_pe_tp(
                                [(wTa[:, j, 2 * ch * 128:(2 * ch + 2) * 128],
                                  [wq3[:, r2, j * 128:(j + 1) * 128] for r2 in range(2)])
                                 for j in range(4, NCT)], ps_tp5t)
                            if debug and dbgk is not None:
                                p.wait("gpsimd", sV, sV.n)
                                p.wait("gpsimd", dS, dS.n)
                                p.do("gpsimd", lambda e, ch=ch, dbgk=dbgk, wq3=wq3: e.dma_start(
                                    out=dbg[dbgk].rearrange("(r p) c -> p r c", p=128)[:, 2 * ch:2 * ch + 2, :],
                                    in_=wq3), dG, 16)
                    if pend_tp is not None:
                        pend_tp()

                ps_tp5.__exit__(None, None, None)
                with nc.sbuf_tensor("hbuf", [128, 2 * I_FF], F32) as hbuf, \
                     nc.sbuf_tensor("sg", [128, 1024], F32) as sg, \
                     nc.sbuf_tensor("hq", [128, 2 * I_FF], BF16) as hq, \
                     nc.sbuf_tensor("hqT", [128, 2 * I_FF], BF16) as hqT, \
                     nc.psum_tensor("ps_g", [128, 2 * 512], F32) as ps_g, \
                     nc.psum_tensor("ps_u", [128, 2 * 512], F32) as ps_u:

                    ic_chunks = [(0, 512), (512, 512), (1024, 512), (1536, 512),
                                 (2048, 512), (2560, 256)]
                    p.wait("tensor", sV, wT_done)
                    p.wait("tensor", tW[0])
                    p.wait("tensor", tW[1])
                    sV_sg = {0: sV.n, 1: sV.n}
                    hqTa2 = hqT.ap().rearrange("p (q x) -> p q x", q=2)
                    for tt in range(NTT + 1):
                        qst = None
                        if tt >= 1:
                            tp_ = tt - 1
                            par = tp_ % 2
                            p.wait("vector", dHS[par])
                            p.wait("vector", tH[par])
                            qst = quant_stages(
                                hbuf.ap()[:, par * I_FF:(par + 1) * I_FF], I_FF, qb,
                                hq.ap()[:, par * I_FF:(par + 1) * I_FF])
                        if tt < NTT:
                            hpar = tt % 2
                            for ci, (i0, ilen) in enumerate(ic_chunks):
                                g = ci % 2
                                p.wait("tensor", sV, sV_sg[g])
                                for i in range(NCT):
                                    p.do("tensor", lambda e, tt=tt, i=i, i0=i0, ilen=ilen, g=g:
                                         e.matmul(out=ps_g.ap()[:, g * 512:g * 512 + ilen],
                                                  lhsT=x1qTa[:, i, tt * 128:(tt + 1) * 128],
                                                  rhs=wgTa[:, i, i0:i0 + ilen],
                                                  start=(i == 0), stop=(i == NCT - 1)))
                                for i in range(NCT):
                                    p.do("tensor", lambda e, tt=tt, i=i, i0=i0, ilen=ilen, g=g:
                                         e.matmul(out=ps_u.ap()[:, g * 512:g * 512 + ilen],
                                                  lhsT=x1qTa[:, i, tt * 128:(tt + 1) * 128],
                                                  rhs=wuTa[:, i, i0:i0 + ilen],
                                                  start=(i == 0), stop=(i == NCT - 1)),
                                         sT if i == NCT - 1 else None, 1)
                                p.wait("scalar", sT, sT.n)
                                p.wait("scalar", sV, sV_sg[g])
                                p.do("scalar", lambda e, ilen=ilen, g=g: e.activation(
                                    out=sg[:, g * 512:g * 512 + ilen],
                                    in_=ps_g.ap()[:, g * 512:g * 512 + ilen],
                                    func=AF.Silu), sA, 1)
                                p.wait("vector", sA, sA.n)
                                p.do("vector", lambda e, i0=i0, ilen=ilen, g=g, hpar=hpar:
                                     e.tensor_tensor(
                                         out=hbuf.ap()[:, hpar * I_FF + i0:hpar * I_FF + i0 + ilen],
                                         in0=sg[:, g * 512:g * 512 + ilen],
                                         in1=ps_u.ap()[:, g * 512:g * 512 + ilen],
                                         op=ALU.mult), sV, 1)
                                sV_sg[g] = sV.n
                                if qst is not None:
                                    qst[ci]()
                        if tt >= 1:
                            if tt == NTT:
                                for st in qst:
                                    st()
                            tp = tt - 1
                            par = tp % 2
                            p.wait("sync", sV, sV.n)
                            p.wait("sync", dHS[par])
                            spread_transposes(
                                [(hqTa2[:, par, it * 128:(it + 1) * 128],
                                  hq.ap()[:, par * I_FF + it * 128:par * I_FF + (it + 1) * 128])
                                 for it in range(NIT)], tH[par])
                            p.wait("gpsimd", tH[par])
                            p.do("gpsimd", lambda e, tp=tp, par=par: e.dma_start(
                                out=hqT_dram[tp, :, :], in_=hqTa2[:, par, :]), dHS[par], 16)
                            if debug:
                                p.wait("gpsimd", sV, sV.n)
                                p.do("gpsimd", lambda e, tp=tp, par=par: e.dma_start(
                                    out=dbg["d_hq"].rearrange("(t p) i -> p t i", p=128)[:, tp, :],
                                    in_=hq.ap()[:, par * I_FF:(par + 1) * I_FF]), dG, 16)
        with nc.sbuf_tensor("wdT", [128, NIT * D], BF16) as wdT, \
             nc.sbuf_tensor("wdq_sb", [128, 2 * I_FF], BF16) as wdq_sb, \
             nc.sbuf_tensor("hin", [128, 2 * I_FF], BF16) as hin, \
             nc.sbuf_tensor("x1in", [128, 2 * D], F32) as x1in, \
             nc.sbuf_tensor("otile", [128, 2 * D], F32) as otile, \
             nc.sbuf_tensor("sqs2", [128, D], BF16) as sqs2, \
             nc.psum_tensor("ps_o", [128, 2 * D], F32) as ps_o, \
             nc.psum_tensor("ps_tp7", [128, 2048], BF16) as ps_tp7:

            wdTa = wdT.ap().rearrange("p (it d) -> p it d", d=D)
            wdq_r2 = wdq_dram.rearrange("(r p) i -> p r i", p=128)
            for s_ in (dWQ[0], dWQ[1], dHS[0], dHS[1]):
                        p.wait("sync", s_)
            wdq_reads = [0, 0]
            wdT_done = 0
            for rt in range(8):
                par = rt % 2
                p.wait("sync", sT, wdq_reads[par])
                p.wait("sync", dS, dS.n)
                p.do("sync", lambda e, rt=rt, par=par: e.dma_start(
                    out=wdq_sb.ap()[:, par * I_FF:(par + 1) * I_FF],
                    in_=wdq_r2[:, rt, :]), dS, 16)
                p.wait("tensor", dS, dS.n)
                it_groups = [(0, 4), (4, 4), (8, 4), (12, 4), (16, 4), (20, 2)]
                wdq_reads[par], wdT_done = emit_pe_tp(
                    [(wdTa[:, it0:it0 + n, rt * 128:(rt + 1) * 128],
                      [wdq_sb.ap()[:, par * I_FF + it * 128:par * I_FF + (it + 1) * 128]
                       for it in range(it0, it0 + n)])
                     for (it0, n) in it_groups], ps_tp7)
            p.wait("tensor", sV, wdT_done)

            hina = hin.ap().rearrange("p (g x) -> p g x", x=I_FF)
            x1i = x1in.ap().rearrange("p (g d) -> p g d", d=D)
            oti = otile.ap().rearrange("p (g d) -> p g d", d=D)
            pso = ps_o.ap().rearrange("p (g d) -> p g d", d=D)
            x1_rt = x1_dram.rearrange("(t p) d -> p t d", p=128)
            out_rt = out_hbm.rearrange("(t p) d -> p t d", p=128)

            cons_done = {0: 0, 1: 0}
            evac_done = {0: 0, 1: 0}
            store_done = {0: 0, 1: 0}
            for tt in range(NTT):
                g = tt % 2
                if cons_done[g]:
                    p.wait("gpsimd", sT, cons_done[g])
                if evac_done[g]:
                    p.wait("sync", sV, evac_done[g])
                p.wait("gpsimd", dG, dG.n)
                p.do("gpsimd", lambda e, tt=tt, g=g: e.dma_start(
                    out=hina[:, g, :], in_=hqT_dram[tt, :, :]), dG, 16)
                dG_here = dG.n
                p.wait("sync", dS, dS.n)
                p.do("sync", lambda e, tt=tt, g=g: e.dma_start(
                    out=x1i[:, g, :], in_=x1_rt[:, tt, :]), dS, 16)
                dS_here = dS.n
                p.wait("tensor", dG, dG_here)
                p.wait("tensor", dS, dS_here)
                if evac_done[g]:
                    p.wait("tensor", sV, evac_done[g])
                hin3 = hina[:, g, :].rearrange("p (it t) -> p it t", t=128)
                for it in range(NIT):
                    for c in range(2):
                        p.do("tensor", lambda e, it=it, c=c, g=g, hin3=hin3: e.matmul(
                            out=pso[:, g, c * 512:(c + 1) * 512],
                            lhsT=hin3[:, it, :],
                            rhs=wdTa[:, it, c * 512:(c + 1) * 512],
                            start=(it == 0), stop=(it == NIT - 1)),
                            sT if (it == NIT - 1 and c == 1) else None, 1)
                cons_done[g] = sT.n
                p.wait("vector", sT, sT.n)
                if store_done[g]:
                    p.wait("vector", dOS[g], store_done[g])
                p.do("vector", lambda e, g=g: e.tensor_tensor(
                    out=oti[:, g, :], in0=pso[:, g, :], in1=x1i[:, g, :], op=ALU.add), sV, 1)
                evac_done[g] = sV.n
                p.wait("vector", sV)
                p.do("vector", lambda e, g=g: e.tensor_reduce(
                    out=small[:, 0:1], in_=oti[:, g, :], axis=AX.X, op=ALU.add), sV, 1)
                p.wait("scalar", sV, sV.n)
                p.do("scalar", lambda e, g=g: e.activation(
                    out=sqs2[:, :], in_=oti[:, g, :], func=AF.Square,
                    accum_out=small[:, 1:2]), sA, 1)
                p.wait("vector", sA, sA.n)
                p.chain("vector", lambda e: e.tensor_scalar(
                    out=small[:, 2:3], in0=small[:, 0:1], scalar1=1.0 / D, scalar2=None,
                    op0=ALU.mult), sV)
                p.chain("vector", lambda e: e.tensor_scalar(
                    out=small[:, 3:4], in0=small[:, 1:2], scalar1=1.0 / D, scalar2=EPS,
                    op0=ALU.mult, op1=ALU.add), sV)
                p.chain("vector", lambda e: e.scalar_tensor_tensor(
                    out=small[:, 4:5], in0=small[:, 2:3], scalar=small.ap()[:, 2:3],
                    in1=small[:, 3:4], op0=ALU.mult, op1=ALU.subtract), sV)
                p.wait("scalar", sV, sV.n)
                p.chain("scalar", lambda e: e.activation(
                    out=small[:, 5:6], in_=small[:, 4:5], func=AF.Ln, scale=-1.0), sA)
                p.chain("scalar", lambda e: e.activation(
                    out=small[:, 6:7], in_=small[:, 5:6], func=AF.Exp, scale=-0.5), sA)
                p.wait("vector", sA, sA.n)
                p.do("vector", lambda e: e.scalar_tensor_tensor(
                    out=small[:, 7:8], in0=small[:, 2:3], scalar=-1.0,
                    in1=small[:, 6:7], op0=ALU.mult, op1=ALU.mult), sV, 1)
                p.wait("scalar", sV, sV.n)
                p.chain("scalar", lambda e, g=g: e.activation(
                    out=oti[:, g, :], in_=oti[:, g, :], func=AF.Identity,
                    scale=small.ap()[:, 6:7], bias=small.ap()[:, 7:8]), sA)
                p.wait("scalar", sA, sA.n)
                p.do("scalar", lambda e, tt=tt, g=g: e.dma_start(
                    out=out_rt[:, tt, :], in_=oti[:, g, :]), dOS[g], 16)
                store_done[g] = dOS[g].n

            p.wait("sync", dOS[0])
            p.wait("sync", dOS[1])
            if debug:
                p.wait("sync", dG, dG.n)

        with nc.Block() as block:
            @block.tensor
            def _(e):
                for fn in p.ops["tensor"]:
                    fn(e)

            @block.vector
            def _(e):
                for fn in p.ops["vector"]:
                    fn(e)

            @block.scalar
            def _(e):
                for fn in p.ops["scalar"]:
                    fn(e)

            @block.gpsimd
            def _(e):
                for fn in p.ops["gpsimd"]:
                    fn(e)

            @block.sync
            def _(e):
                for fn in p.ops["sync"]:
                    fn(e)

    return nc


def kernel(L0_attn_x, L0_attn_Wq, L0_attn_bq, L0_attn_Wk, L0_attn_bk,
           L0_attn_Wv, L0_attn_bv, L0_attn_Wo, L0_attn_bo, L0_attn_alpha,
           L0_ffn_w_gate, L0_ffn_w_up, L0_ffn_w_down):
    x = np.ascontiguousarray(np.asarray(L0_attn_x, dtype=np.float32))
    B = x.shape[0]
    f32 = lambda a: np.ascontiguousarray(np.asarray(a, dtype=np.float32))
    ins = {
        "wqT": f32(np.asarray(L0_attn_Wq).T), "wkT": f32(np.asarray(L0_attn_Wk).T),
        "wvT": f32(np.asarray(L0_attn_Wv).T), "woT": f32(np.asarray(L0_attn_Wo).T),
        "bq_pp": f32(np.asarray(L0_attn_bq).reshape(NCT, 128).T),
        "bk_pp": f32(np.asarray(L0_attn_bk).reshape(NCT, 128).T),
        "bv_row": f32(np.asarray(L0_attn_bv).reshape(1, D)),
        "bo_row": f32(np.asarray(L0_attn_bo).reshape(1, D)),
        "alpha": f32(np.asarray(L0_attn_alpha).reshape(1, 1)),
        "wg": f32(L0_ffn_w_gate), "wu": f32(L0_ffn_w_up), "wd": f32(L0_ffn_w_down),
    }
    nc = build_program(debug=False)
    in_maps = [dict(ins, x=np.ascontiguousarray(x[b])) for b in range(B)]
    res = run_bass_kernel_spmd(nc, in_maps, core_ids=list(range(B)))
    out = np.stack([res.results[b]["out"] for b in range(B)], axis=0)
    return out.astype(np.float32)


# version 33 (final)
# speedup vs baseline: 1.0014x; 1.0014x over previous
import numpy as np
import concourse.bass as bass
import concourse.mybir as mybir
from concourse.bass_utils import run_bass_kernel_spmd
from contextlib import ExitStack

F32 = mybir.dt.float32
BF16 = mybir.dt.bfloat16
I16 = mybir.dt.int16
I8 = mybir.dt.int8
AF = mybir.ActivationFunctionType
ALU = mybir.AluOpType
AX = mybir.AxisListType

T = 2048
D = 1024
NP = 8
NTT = T // 128
NCT = D // 128
QC = 512
NQC = T // QC
I_FF = 2816
NIT = I_FF // 128
EPS = 1e-5
M_HALF = 6291456.0
M_ONE = 12582912.0
M_TWO = 25165824.0


class Sem:
    def __init__(self, nc, es, name):
        self.h = es.enter_context(nc.semaphore(name))
        self.n = 0


class Prog:
    ENGS = ("tensor", "vector", "scalar", "gpsimd", "sync")

    def __init__(self):
        self.ops = {e: [] for e in self.ENGS}

    def wait(self, eng, sem, val=None):
        v = sem.n if val is None else val
        if v <= 0:
            return
        self.ops[eng].append(lambda e, s=sem.h, v=v: e.wait_ge(s, v))

    def do(self, eng, fn, sem=None, k=1):
        if sem is not None:
            sem.n += k
            self.ops[eng].append(lambda e, f=fn, s=sem.h, kk=k: f(e).then_inc(s, kk))
        else:
            self.ops[eng].append(fn)

    def chain(self, eng, fn, sem):
        self.wait(eng, sem)
        self.do(eng, fn, sem, 1)


def build_program(debug=False):
    nc = bass.Bass("TRN2", target_bir_lowering=False)

    x_hbm = nc.dram_tensor("x", [T, D], F32, kind="ExternalInput").ap()
    wqT = nc.dram_tensor("wqT", [D, D], F32, kind="ExternalInput").ap()
    wkT = nc.dram_tensor("wkT", [D, D], F32, kind="ExternalInput").ap()
    wvT = nc.dram_tensor("wvT", [D, D], F32, kind="ExternalInput").ap()
    woT_h = nc.dram_tensor("woT", [D, D], F32, kind="ExternalInput").ap()
    bq_h = nc.dram_tensor("bq_pp", [128, NCT], F32, kind="ExternalInput").ap()
    bk_h = nc.dram_tensor("bk_pp", [128, NCT], F32, kind="ExternalInput").ap()
    bv_h = nc.dram_tensor("bv_row", [1, D], F32, kind="ExternalInput").ap()
    bo_h = nc.dram_tensor("bo_row", [1, D], F32, kind="ExternalInput").ap()
    alpha_h = nc.dram_tensor("alpha", [1, 1], F32, kind="ExternalInput").ap()
    wg_h = nc.dram_tensor("wg", [I_FF, D], F32, kind="ExternalInput").ap()
    wu_h = nc.dram_tensor("wu", [I_FF, D], F32, kind="ExternalInput").ap()
    wd_h = nc.dram_tensor("wd", [D, I_FF], F32, kind="ExternalInput").ap()
    out_hbm = nc.dram_tensor("out", [T, D], F32, kind="ExternalOutput").ap()
    x1_dram = nc.dram_tensor("x1_scr", [T, D], F32, kind="Internal").ap()
    wdq_dram = nc.dram_tensor("wdq_scr", [D, I_FF], BF16, kind="Internal").ap()
    wgq_dram = nc.dram_tensor("wgq_scr", [I_FF, D], BF16, kind="Internal").ap()
    hqT_dram = nc.dram_tensor("hqT_scr", [NTT, 128, I_FF], BF16, kind="Internal").ap()

    dbg = {}
    if debug:
        for nm, shp, dt in (("d_qt", [D, T], BF16), ("d_kt", [D, T], BF16),
                            ("d_v", [T, D], BF16), ("d_attnT", [D, T], BF16),
                            ("d_x1", [T, D], F32), ("d_x1q", [T, D], BF16),
                            ("d_wgq", [I_FF, D], BF16), ("d_hq", [T, I_FF], BF16)):
            dbg[nm] = nc.dram_tensor(nm, shp, dt, kind="ExternalOutput").ap()

    p = Prog()

    with ExitStack() as es:
        dS = Sem(nc, es, "dS")
        dG = Sem(nc, es, "dG")
        dA = Sem(nc, es, "dA")
        dV = Sem(nc, es, "dV")
        tXT = Sem(nc, es, "tXT")
        tX1 = [Sem(nc, es, "tX1a"), Sem(nc, es, "tX1b")]
        tW = [Sem(nc, es, "tWa"), Sem(nc, es, "tWb")]
        tH = [Sem(nc, es, "tHa"), Sem(nc, es, "tHb")]
        tD = [Sem(nc, es, "tDa"), Sem(nc, es, "tDb")]
        dXR = [Sem(nc, es, "dXRa"), Sem(nc, es, "dXRb"), Sem(nc, es, "dXRc")]
        dXS = [Sem(nc, es, "dXSa"), Sem(nc, es, "dXSb"), Sem(nc, es, "dXSc")]
        dOS = [Sem(nc, es, "dOSa"), Sem(nc, es, "dOSb")]
        dWQ = [Sem(nc, es, "dWQa"), Sem(nc, es, "dWQb")]
        dHS = [Sem(nc, es, "dHSa"), Sem(nc, es, "dHSb")]
        sT = Sem(nc, es, "sT")
        sV = Sem(nc, es, "sV")
        sA = Sem(nc, es, "sA")
        sG = Sem(nc, es, "sG")

        bqs = es.enter_context(nc.sbuf_tensor("bqs", [128, NCT], F32))
        bks = es.enter_context(nc.sbuf_tensor("bks", [128, NCT], F32))
        bvb = es.enter_context(nc.sbuf_tensor("bvb", [1, D], BF16))
        bob = es.enter_context(nc.sbuf_tensor("bob", [1, D], BF16))
        ones_bf = es.enter_context(nc.sbuf_tensor("ones_bf", [1, 128], BF16))
        ones64 = es.enter_context(nc.sbuf_tensor("ones64", [128, 64], BF16))
        ones_f = es.enter_context(nc.sbuf_tensor("ones_f", [1, 128], F32))
        alpha_sb = es.enter_context(nc.sbuf_tensor("alpha_sb", [1, 1], F32))
        aneg = es.enter_context(nc.sbuf_tensor("aneg", [128, 1], F32))
        small = es.enter_context(nc.sbuf_tensor("small", [128, 16], F32))
        junk_t = es.enter_context(nc.sbuf_tensor("junk_t", [128, 16], BF16))
        idn = es.enter_context(nc.sbuf_tensor("idn", [128, 128], BF16))
        idnf = es.enter_context(nc.sbuf_tensor("idnf", [128, 128], F32))

        def quant_stages(src_ap, F, q, out_ap):
            F2 = F // 2
            nb2 = F2 // 16
            halves = []
            for h in range(2):
                c0 = h * F2
                r = q["r"][:, h * nb2:(h + 1) * nb2]
                s = q["s"][:, h * nb2:(h + 1) * nb2]
                halves.append(dict(
                    src3=src_ap[:, c0:c0 + F2].rearrange("p (b k) -> p b k", k=16),
                    r=r, s=s,
                    rb=r.unsqueeze(2).broadcast_to([128, nb2, 16]),
                    sb=s.unsqueeze(2).broadcast_to([128, nb2, 16]),
                    xn=q["xn"][:, c0:c0 + F2], c1=q["c1"][:, c0:c0 + F2],
                    c2=q["c2"][:, c0:c0 + F2], cc=q["cc"][:, c0:c0 + F2],
                    m=q["m"][:, c0:c0 + F2],
                    xn3=q["xn"][:, c0:c0 + F2].rearrange("p (b k) -> p b k", k=16),
                    c13=q["c1"][:, c0:c0 + F2].rearrange("p (b k) -> p b k", k=16),
                    out3=out_ap[:, c0:c0 + F2].rearrange("p (b k) -> p b k", k=16)))
            C = lambda fn: p.chain("vector", fn, sV)
            A = lambda fn: p.chain("scalar", fn, sA)
            a_sq1, a_sq2 = {}, {}

            def front(h):
                d = halves[h]
                C(lambda e, d=d: e.tensor_reduce(
                    out=d["r"], in_=d["src3"], axis=AX.X, op=ALU.max,
                    apply_absolute_value=True))
                C(lambda e, d=d: e.tensor_scalar(
                    out=d["s"], in0=d["r"], scalar1=1.0 / 6.0, scalar2=1e-12,
                    op0=ALU.mult, op1=ALU.max))
                C(lambda e, d=d: e.reciprocal(out=d["r"], in_=d["s"]))
                C(lambda e, d=d: e.tensor_tensor(
                    out=d["xn3"], in0=d["src3"], in1=d["rb"], op=ALU.mult))
                C(lambda e, d=d: e.tensor_scalar(
                    out=d["c1"], in0=d["xn"], scalar1=M_HALF, scalar2=M_HALF,
                    op0=ALU.add, op1=ALU.subtract))
                p.wait("scalar", sV)
                A(lambda e, d=d: e.activation(out=d["cc"], in_=d["c1"], func=AF.Square))
                a_sq1[h] = sA.n
                C(lambda e, d=d: e.tensor_scalar(
                    out=d["c2"], in0=d["xn"], scalar1=M_ONE, scalar2=M_ONE,
                    op0=ALU.add, op1=ALU.subtract))

            def mid(h):
                d = halves[h]
                p.wait("vector", sA, a_sq1[h])
                C(lambda e, d=d: e.tensor_scalar(
                    out=d["m"], in0=d["cc"], scalar1=6.25, scalar2=None, op0=ALU.is_ge))
                p.wait("scalar", sV)
                A(lambda e, d=d: e.activation(out=d["cc"], in_=d["c2"], func=AF.Square))
                a_sq2[h] = sA.n
                C(lambda e, d=d: e.copy_predicated(out=d["c1"], mask=d["m"], data=d["c2"]))

            def back(h):
                d = halves[h]
                p.wait("vector", sA, a_sq2[h])
                C(lambda e, d=d: e.tensor_scalar(
                    out=d["m"], in0=d["cc"], scalar1=25.0, scalar2=None, op0=ALU.is_ge))
                C(lambda e, d=d: e.tensor_scalar(
                    out=d["c2"], in0=d["xn"], scalar1=M_TWO, scalar2=M_TWO,
                    op0=ALU.add, op1=ALU.subtract))
                C(lambda e, d=d: e.copy_predicated(out=d["c1"], mask=d["m"], data=d["c2"]))
                C(lambda e, d=d: e.tensor_tensor(
                    out=d["out3"], in0=d["c13"], in1=d["sb"], op=ALU.mult))

            return [lambda: front(0), lambda: front(1), lambda: mid(0),
                    lambda: mid(1), lambda: back(0), lambda: back(1)]

        def emit_quant(src_ap, F, q, out_ap):
            for st in quant_stages(src_ap, F, q, out_ap):
                st()

        QUEUES = ("scalar", "sync")
        QSEM = {"scalar": dA, "gpsimd": dG, "sync": dS, "vector": dV}

        def spread_transposes(pairs, sem):
            for (o, i) in pairs:
                p.do("sync", lambda e, o=o, i=i: e.dma_start_transpose(out=o, in_=i),
                     sem, 16)
            return sem.n

        pe_tp_state = {}

        def emit_pe_tp(groups, ps_tp):
            slot_free = pe_tp_state.setdefault(id(ps_tp), [0, 0])
            for gi, (dst, srcs) in enumerate(groups):
                sl = gi % 2
                p.wait("tensor", sV, slot_free[sl])
                for si, s_ in enumerate(srcs):
                    p.do("tensor", lambda e, sl=sl, si=si, s_=s_: e.transpose(
                        out=ps_tp.ap()[:, sl * 1024 + si * 128:sl * 1024 + (si + 1) * 128],
                        in_=s_, identity=idn[:, :]), sT, 1)
                p.wait("vector", sT, sT.n)
                p.do("vector", lambda e, sl=sl, dst=dst, n=len(srcs): e.tensor_copy(
                    out=dst, in_=ps_tp.ap()[:, sl * 1024:sl * 1024 + n * 128]), sV, 1)
                slot_free[sl] = sV.n
            return sT.n, sV.n

        def wait_counts(eng, counts):
            for qn, v in counts.items():
                p.wait(eng, QSEM[qn], v)

        p.do("sync", lambda e: e.dma_start(out=bqs[:, :], in_=bq_h[:, :]), dS, 16)
        p.do("sync", lambda e: e.dma_start(out=bks[:, :], in_=bk_h[:, :]), dS, 16)
        p.do("sync", lambda e: e.dma_start(out=alpha_sb[:, :], in_=alpha_h[:, :]), dS, 16)
        dS_small = dS.n
        p.do("gpsimd", lambda e: e.dma_start(out=bvb[:, :], in_=bv_h[:, :]), dG, 16)
        p.do("gpsimd", lambda e: e.dma_start(out=bob[:, :], in_=bo_h[:, :]), dG, 16)
        p.do("vector", lambda e: e.memset(ones_bf[:, :], 1.0))
        p.do("vector", lambda e: e.memset(ones64[:, :], 1.0))
        p.do("vector", lambda e: e.memset(junk_t[:, :], 0.0))
        p.do("vector", lambda e: e.memset(ones_f[:, :], 1.0), sV, 1)
        sV_ones = sV.n
        p.do("gpsimd", lambda e: e.iota(
            out=idnf[:, :], pattern=[[-1, 128]], base=0, channel_multiplier=1,
            allow_small_or_imprecise_dtypes=True), sG, 1)
        p.wait("vector", sG, sG.n)
        p.do("vector", lambda e: e.tensor_scalar(
            out=idn[:, :], in0=idnf[:, :], scalar1=0.0, scalar2=None,
            op0=ALU.is_equal), sV, 1)
        sV_idn = sV.n
        p.wait("vector", dS, dS_small)
        p.do("vector", lambda e: e.tensor_scalar(
            out=bqs[:, :], in0=bqs[:, :], scalar1=0.125, scalar2=None, op0=ALU.mult), sV, 1)

        with nc.sbuf_tensor("QT", [128, NCT * T], BF16) as QT, \
             nc.sbuf_tensor("KT", [128, NCT * T], BF16) as KT, \
             nc.sbuf_tensor("VV", [128, NTT * D], BF16) as VV, \
             nc.sbuf_tensor("Wd", [128, 3968], F32) as Wd:

            QTa = QT.ap().rearrange("p (c t) -> p c t", t=T)
            KTa = KT.ap().rearrange("p (c t) -> p c t", t=T)
            VVa = VV.ap().rearrange("p (k d) -> p k d", d=D)

            with nc.sbuf_tensor("XT", [128, NCT * T], BF16) as XT:
                XTa = XT.ap().rearrange("p (c t) -> p c t", t=T)
                with nc.sbuf_tensor("wbuf", [128, NCT * D], BF16) as wbuf, \
                     nc.sbuf_tensor("xb", [128, NTT * D], BF16) as xb:
                    ps_al_cm = nc.psum_tensor("ps_al", [128, 1], F32)
                    ps_al = ps_al_cm.__enter__()
                    xba = xb.ap().rearrange("p (t d) -> p t d", d=D)
                    x_r = x_hbm.rearrange("(t p) d -> p t d", p=128)
                    p.do("gpsimd", lambda e: e.dma_start(out=xba, in_=x_r), dG, 16)
                    dG_xb = dG.n
                    p.do("gpsimd", lambda e: e.iota(
                        out=Wd[:, :], pattern=[[-1, 3968]], base=1920,
                        channel_multiplier=1, allow_small_or_imprecise_dtypes=True), sG, 1)
                    p.wait("tensor", dS, dS_small)
                    p.wait("tensor", sV, sV_ones)
                    p.do("tensor", lambda e: e.matmul(
                        out=ps_al[:, :], lhsT=ones_f[:, :], rhs=alpha_sb[:, :],
                        start=True, stop=True), sT, 1)
                    p.wait("vector", sT, sT.n)
                    p.do("vector", lambda e: e.tensor_scalar(
                        out=aneg[:, :], in0=ps_al[:, :], scalar1=-1.0, scalar2=None,
                        op0=ALU.mult), sV, 1)
                    p.wait("vector", sG, sG.n)
                    p.do("vector", lambda e: e.scalar_tensor_tensor(
                        out=Wd[:, :], in0=Wd[:, :], scalar=-1.0, in1=Wd[:, :],
                        op0=ALU.mult, op1=ALU.max), sV, 1)
                    p.wait("scalar", sV, sV.n)
                    p.do("scalar", lambda e: e.activation(
                        out=Wd[:, :], in_=Wd[:, :], func=AF.Exp,
                        scale=aneg.ap()[:, 0:1]), sA, 1)
                    sA_wd = sA.n
                    sV_prep = sV.n
                    with nc.psum_tensor("ps_tp0", [128, 2048], BF16) as ps_tp0:
                        p.wait("tensor", dG, dG_xb)
                        p.wait("tensor", sV, sV_idn)
                        xt_groups = [(XTa[:, j, tt0 * 128:(tt0 + 4) * 128],
                                      [xba[:, tt, j * 128:(j + 1) * 128]
                                       for tt in range(tt0, tt0 + 4)])
                                     for j in range(NCT) for tt0 in (0, 4, 8, 12)]
                        _, xt_done = emit_pe_tp(xt_groups, ps_tp0)

                    ps_al_cm.__exit__(None, None, None)
                    ph1 = nc.psum_tensor("ps_q", [128, 2 * T], F32)
                    ps_q = ph1.__enter__()
                    wba = wbuf.ap().rearrange("p (c d) -> p c d", d=D)
                    psq = ps_q.ap().rearrange("p (g t) -> p g t", t=T)
                    evac_done = {0: 0, 1: 0}

                    def load_w(src, after_sT):
                        p.wait("gpsimd", sT, after_sT)
                        p.wait("gpsimd", tXT)
                        p.wait("gpsimd", dG, dG.n)
                        wr = src.rearrange("(c p) d -> p c d", p=128)
                        p.do("gpsimd", lambda e, wr=wr: e.dma_start(out=wba, in_=wr), dG, 16)
                        return dG.n

                    dG_w = load_w(wqT, 0)
                    p.wait("tensor", sV, max(sV_prep, xt_done))
                    for which, (bias_sb, scl, outTa) in enumerate(
                            ((bqs, 0.125, QTa), (bks, 1.0, KTa))):
                        p.wait("tensor", dG, dG_w)
                        for j in range(NCT):
                            g = j % 2
                            if evac_done[g]:
                                p.wait("tensor", sA, evac_done[g])
                            for i in range(NCT):
                                for q4 in range(4):
                                    p.do("tensor", lambda e, i=i, j=j, q4=q4, g=g: e.matmul(
                                        out=psq[:, g, q4 * 512:(q4 + 1) * 512],
                                        lhsT=wba[:, i, j * 128:(j + 1) * 128],
                                        rhs=XTa[:, i, q4 * 512:(q4 + 1) * 512],
                                        start=(i == 0), stop=(i == NCT - 1)),
                                        sT if (i == NCT - 1 and q4 == 3) else None, 1)
                            p.wait("scalar", sT, sT.n)
                            p.do("scalar", lambda e, j=j, g=g, b=bias_sb, s=scl, o=outTa:
                                 e.activation(out=o[:, j, :], in_=psq[:, g, :],
                                              func=AF.Identity, bias=b.ap()[:, j:j + 1],
                                              scale=s), sA, 1)
                            evac_done[g] = sA.n
                        dG_w = load_w(wkT if which == 0 else wvT, sT.n)
                    p.wait("tensor", dG, dG_w)
                    for tt in range(NTT):
                        g = tt % 2
                        if evac_done[g]:
                            p.wait("tensor", sA, evac_done[g])
                        for i in range(NCT):
                            for c in range(2):
                                p.do("tensor", lambda e, tt=tt, i=i, c=c, g=g: e.matmul(
                                    out=psq[:, g, c * 512:(c + 1) * 512],
                                    lhsT=XTa[:, i, tt * 128:(tt + 1) * 128],
                                    rhs=wba[:, i, c * 512:(c + 1) * 512],
                                    start=(i == 0), stop=False))
                        for c in range(2):
                            p.do("tensor", lambda e, c=c, g=g: e.matmul(
                                out=psq[:, g, c * 512:(c + 1) * 512],
                                lhsT=ones_bf[:, :], rhs=bvb.ap()[:, c * 512:(c + 1) * 512],
                                start=False, stop=True), sT if c == 1 else None, 1)
                        p.wait("scalar", sT, sT.n)
                        p.do("scalar", lambda e, tt=tt, g=g: e.activation(
                            out=VVa[:, tt, 0:D], in_=psq[:, g, 0:D], func=AF.Identity), sA, 1)
                        evac_done[g] = sA.n
                    ph1.__exit__(None, None, None)
            sA_proj = sA.n

            if debug:
                p.wait("gpsimd", sA, sA_proj)
                p.do("gpsimd", lambda e: e.dma_start(
                    out=dbg["d_qt"].rearrange("(c p) t -> p c t", p=128), in_=QTa), dG, 16)
                p.do("gpsimd", lambda e: e.dma_start(
                    out=dbg["d_kt"].rearrange("(c p) t -> p c t", p=128), in_=KTa), dG, 16)
                p.do("gpsimd", lambda e: e.dma_start(
                    out=dbg["d_v"].rearrange("(t p) d -> p t d", p=128), in_=VVa), dG, 16)

            with nc.sbuf_tensor("SDr1", [128, 8 * 512], F32) as SDr1, \
                 nc.sbuf_tensor("SDr2", [128, 8 * 512], F32) as SDr2, \
                 nc.sbuf_tensor("PT1", [128, NTT * QC], BF16) as PT1, \
                 nc.sbuf_tensor("PT2", [128, NTT * QC], BF16) as PT2, \
                 nc.sbuf_tensor("recD", [128, QC], F32) as recD, \
                 nc.psum_tensor("ps_s1", [128, 3 * 512], F32) as ps_s1, \
                 nc.psum_tensor("ps_s2", [128, 3 * 512], F32) as ps_s2, \
                 nc.psum_tensor("ps_av", [128, 512], F32) as ps_av, \
                 nc.psum_tensor("ps_d", [128, 512], F32) as ps_d:

                p.wait("tensor", sA, sA_proj)
                aq_es = ExitStack()
                aq_sb = lambda n, sh, dt: aq_es.enter_context(nc.sbuf_tensor(n, sh, dt))
                awst = aq_sb("awst", [128, 1408], F32)
                awq = aq_sb("awq", [128, 1408], BF16)
                aqxn = aq_sb("aqxn", [128, 1408], F32)
                aqc1 = aq_sb("aqc1", [128, 1408], BF16)
                aqc2 = aq_sb("aqc2", [128, 1408], BF16)
                aqcc = aq_sb("aqcc", [128, 1408], BF16)
                aqm = aq_sb("aqm", [128, 1408], I8)
                aqr = aq_sb("aqr", [128, 88], F32)
                aqs = aq_sb("aqs", [128, 88], F32)

                aqb = {"r": aqr.ap(), "s": aqs.ap(), "xn": aqxn.ap(),
                       "c1": aqc1.ap(), "c2": aqc2.ap(), "cc": aqcc.ap(),
                       "m": aqm.ap()}
                wd_r2 = wd_h.rearrange("(r p) i -> p r i", p=128)
                wdq_r2 = wdq_dram.rearrange("(r p) i -> p r i", p=128)
                wg_r2 = wg_h.rearrange("(r p) c -> p r c", p=128)
                wgq_r2 = wgq_dram.rearrange("(r p) c -> p r c", p=128)
                aw_chunks = []
                for rt in range(8):
                    for hf in range(2):
                        aw_chunks.append((wd_r2[:, rt, hf * 1408:(hf + 1) * 1408],
                                          wdq_r2[:, rt, hf * 1408:(hf + 1) * 1408],
                                          1408))
                for row in range(8):
                    aw_chunks.append((wg_r2[:, row, :], wgq_r2[:, row, :], 1024))
                aw_load = {}
                aw_read = {}
                aw_store = {}
                p.wait("sync", dG, dG.n)
                p.wait("sync", sT, sT.n)
                p.wait("sync", sV, sV.n)
                aw_sched = {}

                def aw_actions(k):
                    src, dst, Fk = aw_chunks[k]
                    stg = quant_stages(awst.ap()[:, :Fk], Fk, aqb, awq.ap()[:, :Fk])

                    def a_load():
                        if k >= 1:
                            p.wait("sync", sV, aw_read[k - 1])
                        p.wait("sync", dS, dS.n)
                        p.do("sync", lambda e, src=src, Fk=Fk: e.dma_start(
                            out=awst.ap()[:, :Fk], in_=src), dS, 16)
                        aw_load[k] = dS.n

                    def a_f0():
                        p.wait("vector", dS, aw_load[k])
                        stg[0]()

                    def a_f1():
                        stg[1]()
                        aw_read[k] = sV.n

                    def a_b0():
                        if k >= 1:
                            p.wait("vector", dWQ[0], aw_store[k - 1])
                        stg[4]()

                    def a_store():
                        p.wait("gpsimd", sV, sV.n)
                        p.do("gpsimd", lambda e, dst=dst, Fk=Fk: e.dma_start(
                            out=dst, in_=awq.ap()[:, :Fk]), dWQ[0], 16)
                        aw_store[k] = dWQ[0].n

                    return [a_load, a_f0, a_f1, stg[2], stg[3], a_b0, stg[5],
                            a_store]

                for k in range(len(aw_chunks)):
                    acts = aw_actions(k)
                    base = k * 21
                    for oi, off in enumerate((0, 2, 4, 6, 8, 10, 12, 13)):
                        aw_sched[base + off] = acts[oi]
                wd_stride = 3968
                units = [(hp, qc) for hp in range(NP) for qc in range(NQC)]
                sd_kt = {0: 0, 1: 0, 2: 0}
                exp_ring = {0: 0, 1: 0}
                exps_grp = {}
                evac_done = {}
                recd_done = {}
                first_sd = True

                def emit_av_kt(up, kt):
                    hp_, _ = units[up]
                    for half, (PT, rq0) in enumerate(((PT1, 0), (PT2, 64))):
                        h = 2 * hp_ + half
                        p.do("tensor", lambda e, kt=kt, PT=PT, rq0=rq0, h=h: e.matmul(
                            out=ps_av.ap()[rq0:rq0 + 64, :],
                            lhsT=VVa[:, kt, h * 64:(h + 1) * 64],
                            rhs=PT.ap()[:, kt * QC:(kt + 1) * QC],
                            start=(kt == 0), stop=(kt == NTT - 1),
                            skip_group_check=True,
                            tile_position=(0, rq0)))
                        p.do("tensor", lambda e, kt=kt, PT=PT, rq0=rq0: e.matmul(
                            out=ps_d.ap()[rq0:rq0 + 64, :],
                            lhsT=ones64[:, :],
                            rhs=PT.ap()[:, kt * QC:(kt + 1) * QC],
                            start=(kt == 0), stop=(kt == NTT - 1),
                            skip_group_check=True,
                            tile_position=(0, rq0)),
                            sT if half == 1 else None, 1)

                for u in range(len(units) + 1):
                    if u < len(units):
                        hp, qc = units[u]
                        for kt in range(NTT):
                            b2 = kt % 3
                            r8 = (kt // 4) % 2
                            p.wait("tensor", sV, sd_kt[b2])
                            for half, (ps, rq0) in enumerate(((ps_s1, 0), (ps_s2, 64))):
                                p.do("tensor",
                                     lambda e, hp=hp, qc=qc, kt=kt, b2=b2, ps=ps, rq0=rq0:
                                     e.matmul(
                                         out=ps.ap()[:, b2 * 512:b2 * 512 + QC],
                                         lhsT=KTa[rq0:rq0 + 64, hp, kt * 128:(kt + 1) * 128],
                                         rhs=QTa[rq0:rq0 + 64, hp, qc * QC:(qc + 1) * QC],
                                         start=True, stop=True,
                                         tile_position=(rq0, 0)),
                                     sT if half == 1 else None, 1)
                            if u >= 1:
                                up = u - 1
                                p.wait("tensor", sA, exps_grp[(up, kt // 4)])
                                if kt == 0:
                                    p.wait("tensor", sV, evac_done.get(up - 1, 0))
                                    p.wait("tensor", sA, recd_done.get(up - 1, 0))
                                emit_av_kt(up, kt)
                            p.wait("vector", sT, sT.n)
                            if first_sd:
                                p.wait("vector", sA, sA_wd)
                                first_sd = False
                            if exp_ring[r8]:
                                p.wait("vector", sA, exp_ring[r8])
                            s8 = kt % 8
                            for hi, (ps, SDr) in enumerate(((ps_s1, SDr1), (ps_s2, SDr2))):
                                def dve_sd(e, ps=ps, SDr=SDr, b2=b2, s8=s8, kt=kt, qc=qc):
                                    base = 1920 - 128 * kt + QC * qc
                                    wd_ap = bass.AP(Wd.ap().tensor, base,
                                                    [[wd_stride, 128], [1, QC]])
                                    return e.tensor_tensor(
                                        out=SDr.ap()[:, s8 * 512:(s8 + 1) * 512],
                                        in0=ps.ap()[:, b2 * 512:b2 * 512 + QC],
                                        in1=wd_ap, op=ALU.mult)
                                p.do("vector", dve_sd, sV, 1)
                            sd_kt[b2] = sV.n
                            if kt % 4 == 3:
                                g = kt // 4
                                p.wait("scalar", sV, sV.n)
                                p.wait("scalar", sT, sT.n)
                                for hi, (SDr, PT) in enumerate(((SDr1, PT1), (SDr2, PT2))):
                                    p.do("scalar", lambda e, SDr=SDr, PT=PT, g=g, r8=r8:
                                         e.activation(
                                             out=PT.ap()[:, 4 * g * QC:(4 * g + 4) * QC],
                                             in_=SDr.ap()[:, r8 * 2048:(r8 + 1) * 2048],
                                             func=AF.Exp), sA, 1)
                                exp_ring[r8] = sA.n
                                exps_grp[(u, g)] = sA.n
                            act = aw_sched.get(u * NTT + kt)
                            if act is not None:
                                act()
                    if u >= 1:
                        up = u - 1
                        hp_, qc_ = units[up]
                        if u == len(units):
                            p.wait("tensor", sA, exps_grp[(up, 3)])
                            p.wait("tensor", sV, evac_done.get(up - 1, 0))
                            p.wait("tensor", sA, recd_done.get(up - 1, 0))
                            for kt in range(NTT):
                                emit_av_kt(up, kt)
                        p.wait("scalar", sT, sT.n)
                        p.chain("scalar", lambda e: e.activation(
                            out=recD[:, :], in_=ps_d.ap()[:, :], func=AF.Ln), sA)
                        p.chain("scalar", lambda e: e.activation(
                            out=recD[:, :], in_=recD[:, :], func=AF.Exp, scale=-1.0), sA)
                        recd_done[up] = sA.n
                        p.wait("vector", sA, sA.n)
                        p.wait("vector", sT, sT.n)
                        p.do("vector", lambda e, hp_=hp_, qc_=qc_: e.tensor_tensor(
                            out=QTa[:, hp_, qc_ * QC:(qc_ + 1) * QC],
                            in0=ps_av.ap()[:, :], in1=recD[:, :], op=ALU.mult), sV, 1)
                        evac_done[up] = sV.n
                aq_es.close()
            sV_attn = sV.n
            sV_evac_last = evac_done[len(units) - 1]
            if debug:
                p.wait("gpsimd", sV, sV_attn)
                p.do("gpsimd", lambda e: e.dma_start(
                    out=dbg["d_attnT"].rearrange("(c p) t -> p c t", p=128), in_=QTa), dG, 16)

            with nc.sbuf_tensor("woT_sb", [128, NCT * D], BF16) as woT_sb, \
                 nc.sbuf_tensor("xres", [128, 3 * D], F32) as xres, \
                 nc.sbuf_tensor("ra", [128, 3 * D], F32) as ra, \
                 nc.sbuf_tensor("sqsc", [128, D], BF16) as sqsc, \
                 nc.psum_tensor("ps_y", [128, 2 * D], F32) as ps_y:

                woTa = woT_sb.ap().rearrange("p (c d) -> p c d", d=D)
                woT_r = woT_h.rearrange("(c p) d -> p c d", p=128)
                p.wait("gpsimd", sV, sV_attn)
                p.do("gpsimd", lambda e: e.dma_start(out=woTa, in_=woT_r), dG, 16)
                dG_wo = dG.n
                xre = xres.ap().rearrange("p (g d) -> p g d", d=D)
                raa = ra.ap().rearrange("p (g d) -> p g d", d=D)
                psy = ps_y.ap().rearrange("p (g d) -> p g d", d=D)
                x_rt = x_hbm.rearrange("(t p) d -> p t d", p=128)
                x1_rt = x1_dram.rearrange("(t p) d -> p t d", p=128)

                p.wait("tensor", dG, dG_wo)
                p.wait("tensor", sV, sV_evac_last)
                p.wait("sync", sV, sV_attn)
                add_vals = {}
                store_vals = {}
                sq_vals = {}

                def p3_head(tt):
                    s3 = tt % 3
                    g = tt % 2
                    b = 8 * g
                    if tt >= 3:
                        p.wait("sync", sV, add_vals[tt - 3])
                    p.do("sync", lambda e, tt=tt, s3=s3: e.dma_start(
                        out=xre[:, s3, :], in_=x_rt[:, tt, :]), dXR[s3], 16)
                    dS_here = dXR[s3].n
                    if tt >= 2:
                        p.wait("tensor", sV, add_vals[tt - 2])
                    for i in range(NCT):
                        for c in range(2):
                            p.do("tensor", lambda e, tt=tt, i=i, c=c, g=g: e.matmul(
                                out=psy[:, g, c * 512:(c + 1) * 512],
                                lhsT=QTa[:, i, tt * 128:(tt + 1) * 128],
                                rhs=woTa[:, i, c * 512:(c + 1) * 512],
                                start=(i == 0), stop=False))
                    for c in range(2):
                        p.do("tensor", lambda e, c=c, g=g: e.matmul(
                            out=psy[:, g, c * 512:(c + 1) * 512],
                            lhsT=ones_bf[:, :], rhs=bob.ap()[:, c * 512:(c + 1) * 512],
                            start=False, stop=True), sT if c == 1 else None, 1)
                    p.wait("vector", sT, sT.n)
                    p.wait("vector", dXR[s3], dS_here)
                    if tt >= 3:
                        p.wait("vector", dXS[s3], store_vals[tt - 3])
                    p.do("vector", lambda e, g=g, s3=s3: e.tensor_tensor(
                        out=raa[:, s3, :], in0=psy[:, g, :], in1=xre[:, s3, :], op=ALU.add), sV, 1)
                    add_vals[tt] = sV.n
                    p.chain("vector", lambda e, s3=s3, b=b: e.tensor_reduce(
                        out=small[:, b:b + 1], in_=raa[:, s3, :], axis=AX.X, op=ALU.add), sV)
                    p.wait("scalar", sV, sV.n)
                    p.chain("scalar", lambda e, s3=s3, b=b: e.activation(
                        out=sqsc[:, :], in_=raa[:, s3, :], func=AF.Square,
                        accum_out=small[:, b + 1:b + 2]), sA)
                    sq_vals[tt] = sA.n

                def p3_tail(tt):
                    s3 = tt % 3
                    g = tt % 2
                    b = 8 * g
                    p.wait("vector", sA, sq_vals[tt])
                    p.chain("vector", lambda e, b=b: e.tensor_scalar(
                        out=small[:, b + 2:b + 3], in0=small[:, b:b + 1], scalar1=1.0 / D,
                        scalar2=None, op0=ALU.mult), sV)
                    p.chain("vector", lambda e, b=b: e.tensor_scalar(
                        out=small[:, b + 3:b + 4], in0=small[:, b + 2:b + 3], scalar1=-1.0,
                        scalar2=None, op0=ALU.mult), sV)
                    p.chain("vector", lambda e, b=b: e.tensor_scalar(
                        out=small[:, b + 5:b + 6], in0=small[:, b + 1:b + 2], scalar1=1.0 / D,
                        scalar2=EPS, op0=ALU.mult, op1=ALU.add), sV)
                    p.chain("vector", lambda e, b=b: e.scalar_tensor_tensor(
                        out=small[:, b + 4:b + 5], in0=small[:, b + 2:b + 3],
                        scalar=small.ap()[:, b + 2:b + 3],
                        in1=small[:, b + 5:b + 6], op0=ALU.mult, op1=ALU.subtract), sV)
                    p.wait("scalar", sV, sV.n)
                    p.chain("scalar", lambda e, b=b: e.activation(
                        out=small[:, b + 5:b + 6], in_=small[:, b + 4:b + 5], func=AF.Ln,
                        scale=-1.0), sA)
                    p.chain("scalar", lambda e, b=b: e.activation(
                        out=small[:, b + 6:b + 7], in_=small[:, b + 5:b + 6], func=AF.Exp,
                        scale=-0.5), sA)
                    p.chain("scalar", lambda e, b=b: e.activation(
                        out=small[:, b + 7:b + 8], in_=small[:, b + 6:b + 7], func=AF.Identity,
                        scale=small.ap()[:, b + 3:b + 4]), sA)
                    p.chain("scalar", lambda e, s3=s3, b=b: e.activation(
                        out=raa[:, s3, :], in_=raa[:, s3, :], func=AF.Identity,
                        scale=small.ap()[:, b + 6:b + 7], bias=small.ap()[:, b + 7:b + 8]), sA)
                    p.wait("scalar", sA, sA.n)
                    p.do("scalar", lambda e, tt=tt, s3=s3: e.dma_start(
                        out=x1_rt[:, tt, :], in_=raa[:, s3, :]), dXS[s3], 16)
                    store_vals[tt] = dXS[s3].n

                p3_head(0)
                for tt in range(1, NTT):
                    p3_head(tt)
                    p3_tail(tt - 1)
                p3_tail(NTT - 1)
                if debug:
                    for s_ in dXS:
                        p.wait("gpsimd", s_)
                    p.do("gpsimd", lambda e: e.dma_start(
                        out=dbg["d_x1"].rearrange("(t p) d -> p t d", p=128),
                        in_=x1_dram.rearrange("(t p) d -> p t d", p=128)), dG, 16)

        x1qT = es.enter_context(nc.sbuf_tensor("x1qT", [128, NCT * T], BF16))
        qxn = es.enter_context(nc.sbuf_tensor("qxn", [128, I_FF], F32))
        qc1 = es.enter_context(nc.sbuf_tensor("qc1", [128, I_FF], BF16))
        qc2 = es.enter_context(nc.sbuf_tensor("qc2", [128, I_FF], BF16))
        qcc = es.enter_context(nc.sbuf_tensor("qcc", [128, I_FF], BF16))
        qm = es.enter_context(nc.sbuf_tensor("qm", [128, I_FF], I8))
        qr = es.enter_context(nc.sbuf_tensor("qr", [128, I_FF // 16], F32))
        qs = es.enter_context(nc.sbuf_tensor("qs", [128, I_FF // 16], F32))
        if True:
            qb = {"r": qr.ap(), "s": qs.ap(), "xn": qxn.ap(), "c1": qc1.ap(),
                  "c2": qc2.ap(), "cc": qcc.ap(), "m": qm.ap()}
            x1qTa = x1qT.ap().rearrange("p (c t) -> p c t", t=T)
            x1_rt = x1_dram.rearrange("(t p) d -> p t d", p=128)

            with nc.sbuf_tensor("x1sl", [128, 2 * 2 * D], F32) as x1sl, \
                 nc.sbuf_tensor("x1q", [128, 2 * 2 * D], BF16) as x1q:
                x1sla = x1sl.ap().rearrange("p (q t d) -> p q t d", q=2, d=D)
                x1qa = x1q.ap().rearrange("p (q t d) -> p q t d", q=2, d=D)
                for s_ in dXS:
                    p.wait("sync", s_)
                    p.wait("scalar", s_)
                ps_tp4 = nc.psum_tensor("ps_tp4", [128, 2048], BF16)
                ps_tp4t = ps_tp4.__enter__()
                x1q_reads = [0, 0]
                x1qT_done = 0
                for sl in range(NTT // 2):
                    par = sl % 2
                    p.wait("sync", sT, x1q_reads[par])
                    p.wait("sync", dS, dS.n)
                    p.do("sync", lambda e, sl=sl, par=par: e.dma_start(
                        out=x1sla[:, par, :, :],
                        in_=x1_rt[:, 2 * sl:2 * sl + 2, :]), dS, 16)
                    p.wait("vector", dS, dS.n)
                    p.wait("vector", sT, x1q_reads[par])
                    emit_quant(x1sl.ap()[:, par * 2 * D:(par + 1) * 2 * D], 2 * D, qb,
                               x1q.ap()[:, par * 2 * D:(par + 1) * 2 * D])
                    p.wait("tensor", sV, sV.n)
                    x1q_reads[par], x1qT_done = emit_pe_tp(
                        [(x1qTa[:, j, 2 * sl * 128:(2 * sl + 2) * 128],
                          [x1qa[:, par, t2, j * 128:(j + 1) * 128] for t2 in range(2)])
                         for j in range(NCT)], ps_tp4t)
                    if debug:
                        p.wait("gpsimd", sV, sV.n)
                        p.do("gpsimd", lambda e, sl=sl, par=par: e.dma_start(
                            out=dbg["d_x1q"].rearrange("(t p) d -> p t d", p=128)[:, 2 * sl:2 * sl + 2, :],
                            in_=x1qa[:, par, :, :]), dG, 16)

            for eng in ("sync", "vector", "scalar", "gpsimd"):
                p.wait(eng, sV, x1qT_done)
            ps_tp4.__exit__(None, None, None)
            with nc.sbuf_tensor("wgT", [128, NCT * I_FF], BF16) as wgT, \
                 nc.sbuf_tensor("wuT", [128, NCT * I_FF], BF16) as wuT:
                wgTa = wgT.ap().rearrange("p (c i) -> p c i", i=I_FF)
                wuTa = wuT.ap().rearrange("p (c i) -> p c i", i=I_FF)

                with nc.sbuf_tensor("wst", [128, 2 * I_FF], F32) as wst, \
                     nc.sbuf_tensor("wqo", [128, 2 * I_FF], BF16) as wqo:
                    p.wait("sync", dWQ[0])
                    q_end = {0: 0, 1: 0}
                    cidx = 0
                    ps_tp5 = nc.psum_tensor("ps_tp5", [128, 2048], BF16)
                    ps_tp5t = ps_tp5.__enter__()
                    wqo_reads = [0, 0]
                    wqo_dmat = [0, 0]
                    pend_tp = None
                    wT_done = 0
                    wgq_r = wgq_dram.rearrange("(r p) c -> p r c", p=128)
                    for w_src, wTa, preq, dbgk in ((wg_h, wgTa, 4, "d_wgq"),
                                                   (wu_h, wuTa, 0, None)):
                        w_r = w_src.rearrange("(r p) c -> p r c", p=128)
                        for ch in range(11):
                            par = cidx % 2
                            cidx += 1
                            wqop = wqo.ap()[:, par * I_FF:par * I_FF + 2 * D]
                            wq3 = wqop.rearrange("p (r c) -> p r c", c=D)
                            if ch < preq:
                                p.wait("gpsimd", sT, wqo_reads[par])
                                p.wait("gpsimd", tW[par], wqo_dmat[par])
                                p.wait("gpsimd", dG, dG.n)
                                p.do("gpsimd", lambda e, ch=ch, wq3=wq3: e.dma_start(
                                    out=wq3, in_=wgq_r[:, 2 * ch:2 * ch + 2, :]), dG, 16)
                                p.wait("tensor", dG, dG.n)
                            else:
                                p.wait("gpsimd", sV, q_end[par])
                                p.wait("gpsimd", dG, dG.n)
                                p.do("gpsimd", lambda e, ch=ch, w_r=w_r, par=par: e.dma_start(
                                    out=wst.ap()[:, par * I_FF:par * I_FF + 2 * D].rearrange(
                                        "p (r c) -> p r c", c=D),
                                    in_=w_r[:, 2 * ch:2 * ch + 2, :]), dG, 16)
                                p.wait("vector", dG, dG.n)
                                p.wait("vector", sT, wqo_reads[par])
                                p.wait("vector", tW[par], wqo_dmat[par])
                                emit_quant(wst.ap()[:, par * I_FF:par * I_FF + 2 * D],
                                           2 * D, qb, wqop)
                                q_end[par] = sV.n
                                p.wait("tensor", sV, sV.n)
                            if pend_tp is not None:
                                pend_tp()
                            qv = q_end[par]
                            dv = dG.n

                            def mk_pend(par=par, wTa=wTa, wq3=wq3, ch=ch,
                                        qv=qv, dv=dv, direct=(ch < preq)):
                                def go():
                                    if direct:
                                        p.wait("sync", dG, dv)
                                    else:
                                        p.wait("sync", sV, qv)
                                    wqo_dmat[par] = spread_transposes(
                                        [(wTa[:, j, (2 * ch + r2) * 128:
                                               (2 * ch + r2 + 1) * 128],
                                          wq3[:, r2, j * 128:(j + 1) * 128])
                                         for j in range(4) for r2 in range(2)],
                                        tW[par])
                                return go
                            pend_tp = mk_pend()
                            wqo_reads[par], wT_done = emit_pe_tp(
                                [(wTa[:, j, 2 * ch * 128:(2 * ch + 2) * 128],
                                  [wq3[:, r2, j * 128:(j + 1) * 128] for r2 in range(2)])
                                 for j in range(4, NCT)], ps_tp5t)
                            if debug and dbgk is not None:
                                p.wait("gpsimd", sV, sV.n)
                                p.wait("gpsimd", dS, dS.n)
                                p.do("gpsimd", lambda e, ch=ch, dbgk=dbgk, wq3=wq3: e.dma_start(
                                    out=dbg[dbgk].rearrange("(r p) c -> p r c", p=128)[:, 2 * ch:2 * ch + 2, :],
                                    in_=wq3), dG, 16)
                    if pend_tp is not None:
                        pend_tp()

                ps_tp5.__exit__(None, None, None)
                with nc.sbuf_tensor("hbuf", [128, 2 * I_FF], F32) as hbuf, \
                     nc.sbuf_tensor("sg", [128, 1024], F32) as sg, \
                     nc.sbuf_tensor("hq", [128, 2 * I_FF], BF16) as hq, \
                     nc.sbuf_tensor("hqT", [128, 2 * I_FF], BF16) as hqT, \
                     nc.psum_tensor("ps_g", [128, 2 * 512], F32) as ps_g, \
                     nc.psum_tensor("ps_u", [128, 2 * 512], F32) as ps_u:

                    ic_chunks = [(0, 512), (512, 512), (1024, 512), (1536, 512),
                                 (2048, 512), (2560, 256)]
                    p.wait("tensor", sV, wT_done)
                    p.wait("tensor", tW[0])
                    p.wait("tensor", tW[1])
                    sV_sg = {0: sV.n, 1: sV.n}
                    hqTa2 = hqT.ap().rearrange("p (q x) -> p q x", q=2)
                    for tt in range(NTT + 1):
                        qst = None
                        if tt >= 1:
                            tp_ = tt - 1
                            par = tp_ % 2
                            p.wait("vector", dHS[par])
                            p.wait("vector", tH[par])
                            qst = quant_stages(
                                hbuf.ap()[:, par * I_FF:(par + 1) * I_FF], I_FF, qb,
                                hq.ap()[:, par * I_FF:(par + 1) * I_FF])
                        if tt < NTT:
                            hpar = tt % 2
                            for ci, (i0, ilen) in enumerate(ic_chunks):
                                g = ci % 2
                                p.wait("tensor", sV, sV_sg[g])
                                for i in range(NCT):
                                    p.do("tensor", lambda e, tt=tt, i=i, i0=i0, ilen=ilen, g=g:
                                         e.matmul(out=ps_g.ap()[:, g * 512:g * 512 + ilen],
                                                  lhsT=x1qTa[:, i, tt * 128:(tt + 1) * 128],
                                                  rhs=wgTa[:, i, i0:i0 + ilen],
                                                  start=(i == 0), stop=(i == NCT - 1)))
                                for i in range(NCT):
                                    p.do("tensor", lambda e, tt=tt, i=i, i0=i0, ilen=ilen, g=g:
                                         e.matmul(out=ps_u.ap()[:, g * 512:g * 512 + ilen],
                                                  lhsT=x1qTa[:, i, tt * 128:(tt + 1) * 128],
                                                  rhs=wuTa[:, i, i0:i0 + ilen],
                                                  start=(i == 0), stop=(i == NCT - 1)),
                                         sT if i == NCT - 1 else None, 1)
                                p.wait("scalar", sT, sT.n)
                                p.wait("scalar", sV, sV_sg[g])
                                p.do("scalar", lambda e, ilen=ilen, g=g: e.activation(
                                    out=sg[:, g * 512:g * 512 + ilen],
                                    in_=ps_g.ap()[:, g * 512:g * 512 + ilen],
                                    func=AF.Silu), sA, 1)
                                p.wait("vector", sA, sA.n)
                                p.do("vector", lambda e, i0=i0, ilen=ilen, g=g, hpar=hpar:
                                     e.tensor_tensor(
                                         out=hbuf.ap()[:, hpar * I_FF + i0:hpar * I_FF + i0 + ilen],
                                         in0=sg[:, g * 512:g * 512 + ilen],
                                         in1=ps_u.ap()[:, g * 512:g * 512 + ilen],
                                         op=ALU.mult), sV, 1)
                                sV_sg[g] = sV.n
                                if qst is not None:
                                    qst[ci]()
                        if tt >= 1:
                            if tt == NTT:
                                for st in qst:
                                    st()
                            tp = tt - 1
                            par = tp % 2
                            p.wait("sync", sV, sV.n)
                            p.wait("sync", dHS[par])
                            spread_transposes(
                                [(hqTa2[:, par, it * 128:(it + 1) * 128],
                                  hq.ap()[:, par * I_FF + it * 128:par * I_FF + (it + 1) * 128])
                                 for it in range(NIT)], tH[par])
                            p.wait("gpsimd", tH[par])
                            p.do("gpsimd", lambda e, tp=tp, par=par: e.dma_start(
                                out=hqT_dram[tp, :, :], in_=hqTa2[:, par, :]), dHS[par], 16)
                            if debug:
                                p.wait("gpsimd", sV, sV.n)
                                p.do("gpsimd", lambda e, tp=tp, par=par: e.dma_start(
                                    out=dbg["d_hq"].rearrange("(t p) i -> p t i", p=128)[:, tp, :],
                                    in_=hq.ap()[:, par * I_FF:(par + 1) * I_FF]), dG, 16)
        with nc.sbuf_tensor("wdT", [128, NIT * D], BF16) as wdT, \
             nc.sbuf_tensor("wdq_sb", [128, 2 * I_FF], BF16) as wdq_sb, \
             nc.sbuf_tensor("hin", [128, 2 * I_FF], BF16) as hin, \
             nc.sbuf_tensor("x1in", [128, 2 * D], F32) as x1in, \
             nc.sbuf_tensor("otile", [128, 2 * D], F32) as otile, \
             nc.sbuf_tensor("sqs2", [128, D], BF16) as sqs2, \
             nc.psum_tensor("ps_o", [128, 2 * D], F32) as ps_o, \
             nc.psum_tensor("ps_tp7", [128, 2048], BF16) as ps_tp7:

            wdTa = wdT.ap().rearrange("p (it d) -> p it d", d=D)
            wdq_r2 = wdq_dram.rearrange("(r p) i -> p r i", p=128)
            for s_ in (dWQ[0], dWQ[1], dHS[0], dHS[1]):
                        p.wait("sync", s_)
            wdq_reads = [0, 0]
            wdT_done = 0
            for rt in range(8):
                par = rt % 2
                p.wait("sync", sT, wdq_reads[par])
                p.wait("sync", dS, dS.n)
                p.do("sync", lambda e, rt=rt, par=par: e.dma_start(
                    out=wdq_sb.ap()[:, par * I_FF:(par + 1) * I_FF],
                    in_=wdq_r2[:, rt, :]), dS, 16)
                p.wait("tensor", dS, dS.n)
                it_groups = [(0, 4), (4, 4), (8, 4), (12, 4), (16, 4), (20, 2)]
                wdq_reads[par], wdT_done = emit_pe_tp(
                    [(wdTa[:, it0:it0 + n, rt * 128:(rt + 1) * 128],
                      [wdq_sb.ap()[:, par * I_FF + it * 128:par * I_FF + (it + 1) * 128]
                       for it in range(it0, it0 + n)])
                     for (it0, n) in it_groups], ps_tp7)
            p.wait("tensor", sV, wdT_done)

            hina = hin.ap().rearrange("p (g x) -> p g x", x=I_FF)
            x1i = x1in.ap().rearrange("p (g d) -> p g d", d=D)
            oti = otile.ap().rearrange("p (g d) -> p g d", d=D)
            pso = ps_o.ap().rearrange("p (g d) -> p g d", d=D)
            x1_rt = x1_dram.rearrange("(t p) d -> p t d", p=128)
            out_rt = out_hbm.rearrange("(t p) d -> p t d", p=128)

            cons_done = {0: 0, 1: 0}
            evac_done = {0: 0, 1: 0}
            store_done = {0: 0, 1: 0}
            for tt in range(NTT):
                g = tt % 2
                if cons_done[g]:
                    p.wait("gpsimd", sT, cons_done[g])
                if evac_done[g]:
                    p.wait("sync", sV, evac_done[g])
                p.wait("gpsimd", dG, dG.n)
                p.do("gpsimd", lambda e, tt=tt, g=g: e.dma_start(
                    out=hina[:, g, :], in_=hqT_dram[tt, :, :]), dG, 16)
                dG_here = dG.n
                p.wait("sync", dS, dS.n)
                p.do("sync", lambda e, tt=tt, g=g: e.dma_start(
                    out=x1i[:, g, :], in_=x1_rt[:, tt, :]), dS, 16)
                dS_here = dS.n
                p.wait("tensor", dG, dG_here)
                p.wait("tensor", dS, dS_here)
                if evac_done[g]:
                    p.wait("tensor", sV, evac_done[g])
                hin3 = hina[:, g, :].rearrange("p (it t) -> p it t", t=128)
                for it in range(NIT):
                    for c in range(2):
                        p.do("tensor", lambda e, it=it, c=c, g=g, hin3=hin3: e.matmul(
                            out=pso[:, g, c * 512:(c + 1) * 512],
                            lhsT=hin3[:, it, :],
                            rhs=wdTa[:, it, c * 512:(c + 1) * 512],
                            start=(it == 0), stop=(it == NIT - 1)),
                            sT if (it == NIT - 1 and c == 1) else None, 1)
                cons_done[g] = sT.n
                p.wait("vector", sT, sT.n)
                if store_done[g]:
                    p.wait("vector", dOS[g], store_done[g])
                p.do("vector", lambda e, g=g: e.tensor_tensor(
                    out=oti[:, g, :], in0=pso[:, g, :], in1=x1i[:, g, :], op=ALU.add), sV, 1)
                evac_done[g] = sV.n
                p.wait("vector", sV)
                p.do("vector", lambda e, g=g: e.tensor_reduce(
                    out=small[:, 0:1], in_=oti[:, g, :], axis=AX.X, op=ALU.add), sV, 1)
                p.wait("scalar", sV, sV.n)
                p.do("scalar", lambda e, g=g: e.activation(
                    out=sqs2[:, :], in_=oti[:, g, :], func=AF.Square,
                    accum_out=small[:, 1:2]), sA, 1)
                p.wait("vector", sA, sA.n)
                p.chain("vector", lambda e: e.tensor_scalar(
                    out=small[:, 2:3], in0=small[:, 0:1], scalar1=1.0 / D, scalar2=None,
                    op0=ALU.mult), sV)
                p.chain("vector", lambda e: e.tensor_scalar(
                    out=small[:, 3:4], in0=small[:, 1:2], scalar1=1.0 / D, scalar2=EPS,
                    op0=ALU.mult, op1=ALU.add), sV)
                p.chain("vector", lambda e: e.scalar_tensor_tensor(
                    out=small[:, 4:5], in0=small[:, 2:3], scalar=small.ap()[:, 2:3],
                    in1=small[:, 3:4], op0=ALU.mult, op1=ALU.subtract), sV)
                p.wait("scalar", sV, sV.n)
                p.chain("scalar", lambda e: e.activation(
                    out=small[:, 5:6], in_=small[:, 4:5], func=AF.Ln, scale=-1.0), sA)
                p.chain("scalar", lambda e: e.activation(
                    out=small[:, 6:7], in_=small[:, 5:6], func=AF.Exp, scale=-0.5), sA)
                p.wait("vector", sA, sA.n)
                p.do("vector", lambda e: e.scalar_tensor_tensor(
                    out=small[:, 7:8], in0=small[:, 2:3], scalar=-1.0,
                    in1=small[:, 6:7], op0=ALU.mult, op1=ALU.mult), sV, 1)
                p.wait("scalar", sV, sV.n)
                p.chain("scalar", lambda e, g=g: e.activation(
                    out=oti[:, g, :], in_=oti[:, g, :], func=AF.Identity,
                    scale=small.ap()[:, 6:7], bias=small.ap()[:, 7:8]), sA)
                p.wait("scalar", sA, sA.n)
                p.do("scalar", lambda e, tt=tt, g=g: e.dma_start(
                    out=out_rt[:, tt, :], in_=oti[:, g, :]), dOS[g], 16)
                store_done[g] = dOS[g].n

            p.wait("sync", dOS[0])
            p.wait("sync", dOS[1])
            if debug:
                p.wait("sync", dG, dG.n)

        with nc.Block() as block:
            @block.tensor
            def _(e):
                for fn in p.ops["tensor"]:
                    fn(e)

            @block.vector
            def _(e):
                for fn in p.ops["vector"]:
                    fn(e)

            @block.scalar
            def _(e):
                for fn in p.ops["scalar"]:
                    fn(e)

            @block.gpsimd
            def _(e):
                for fn in p.ops["gpsimd"]:
                    fn(e)

            @block.sync
            def _(e):
                for fn in p.ops["sync"]:
                    fn(e)

    return nc


def kernel(L0_attn_x, L0_attn_Wq, L0_attn_bq, L0_attn_Wk, L0_attn_bk,
           L0_attn_Wv, L0_attn_bv, L0_attn_Wo, L0_attn_bo, L0_attn_alpha,
           L0_ffn_w_gate, L0_ffn_w_up, L0_ffn_w_down):
    x = np.ascontiguousarray(np.asarray(L0_attn_x, dtype=np.float32))
    B = x.shape[0]
    f32 = lambda a: np.ascontiguousarray(np.asarray(a, dtype=np.float32))
    ins = {
        "wqT": f32(np.asarray(L0_attn_Wq).T), "wkT": f32(np.asarray(L0_attn_Wk).T),
        "wvT": f32(np.asarray(L0_attn_Wv).T), "woT": f32(np.asarray(L0_attn_Wo).T),
        "bq_pp": f32(np.asarray(L0_attn_bq).reshape(NCT, 128).T),
        "bk_pp": f32(np.asarray(L0_attn_bk).reshape(NCT, 128).T),
        "bv_row": f32(np.asarray(L0_attn_bv).reshape(1, D)),
        "bo_row": f32(np.asarray(L0_attn_bo).reshape(1, D)),
        "alpha": f32(np.asarray(L0_attn_alpha).reshape(1, 1)),
        "wg": f32(L0_ffn_w_gate), "wu": f32(L0_ffn_w_up), "wd": f32(L0_ffn_w_down),
    }
    nc = build_program(debug=False)
    in_maps = [dict(ins, x=np.ascontiguousarray(x[b])) for b in range(B)]
    res = run_bass_kernel_spmd(nc, in_maps, core_ids=list(range(B)))
    out = np.stack([res.results[b]["out"] for b in range(B)], axis=0)
    return out.astype(np.float32)
